# version 01 (final)
# baseline (speedup 1.0000x reference)
import contextlib
import numpy as np
import concourse.bass as bass
import concourse.mybir as mybir
from concourse.bass_utils import run_bass_kernel_spmd

F32 = mybir.dt.float32
BF16 = mybir.dt.bfloat16
I32 = mybir.dt.int32
AF = mybir.ActivationFunctionType
ALU = mybir.AluOpType
AX = mybir.AxisListType

ENGINES = ("sync", "tensor", "vector", "scalar", "gpsimd")
_B0 = [0, 1, 4, 5, 8, 9, 12, 13]
_B1 = [2, 3, 6, 7, 10, 11, 14, 15]
QPOS = {}
for _i, _h in enumerate(_B0):
    QPOS[_h] = (_i, 0)
for _i, _h in enumerate(_B1):
    QPOS[_h] = (_i, 64)
QPERM = np.empty(1024, np.int64)
for _h, (_c, _b) in QPOS.items():
    QPERM[_c * 128 + _b: _c * 128 + _b + 64] = np.arange(_h * 64, _h * 64 + 64)
T = 2048
D = 1024
NC = 8
NT4 = T // 512


class Sched:

    def __init__(self, nc, es):
        self.nc = nc
        self.es = es
        self.ops = {e: [] for e in ENGINES}
        self.sem_handles = {}
        self.sem_counts = {}

    def sem(self, name):
        if name not in self.sem_handles:
            self.sem_handles[name] = self.es.enter_context(self.nc.semaphore(name))
            self.sem_counts[name] = 0
        return self.sem_handles[name]

    def emit(self, engine, fn):
        self.ops[engine].append(fn)

    def do(self, engine, fn, inc=None, incn=1):
        if inc is not None:
            h = self.sem(inc)
            self.sem_counts[inc] += incn

            def run(eng, fn=fn, h=h, n=incn):
                ins = fn(eng)
                assert ins is not None, f"instruction fn returned None (sem {h})"
                ins.then_inc(h, n)
            self.emit(engine, run)
        else:
            self.emit(engine, fn)

    def val(self, name):
        self.sem(name)
        return self.sem_counts[name]

    def wait(self, engine, name, value=None):
        h = self.sem(name)
        v = self.sem_counts[name] if value is None else value
        if v <= 0:
            return
        self.emit(engine, lambda eng, h=h, v=v: eng.wait_ge(h, v))

    def build(self):
        nc = self.nc
        with nc.Block() as block:
            for e in ENGINES:
                def runner(eng, e=e):
                    for fn in self.ops[e]:
                        fn(eng)
                getattr(block, e)(runner)
        return nc


def bf(x):
    import ml_dtypes
    return np.asarray(x, dtype=ml_dtypes.bfloat16)



ATTN_A_CONSTS = []


def build_attn_a():
    ATTN_A_CONSTS.clear()
    cmap = {}

    def cidx(v):
        v = float(np.float32(v))
        if v not in cmap:
            cmap[v] = len(ATTN_A_CONSTS)
            ATTN_A_CONSTS.append(v)
        return cmap[v]

    nc = bass.Bass("TRN2", target_bir_lowering=False)
    xTd = nc.dram_tensor("xT", [D, T], F32, kind="ExternalInput").ap()
    wqd = nc.dram_tensor("wqT", [D, 1024], F32, kind="ExternalInput").ap()
    wkd = nc.dram_tensor("wkT", [D, 512], F32, kind="ExternalInput").ap()
    wvd = nc.dram_tensor("wvT", [D, 512], F32, kind="ExternalInput").ap()
    wod = nc.dram_tensor("woT", [D, 1024], F32, kind="ExternalInput").ap()
    bod = nc.dram_tensor("bo2", [128, 8], F32, kind="ExternalInput").ap()
    cbd = nc.dram_tensor("cbt", [128, 352], F32, kind="ExternalInput").ap()
    yTd = nc.dram_tensor("yT", [D, T], F32, kind="ExternalOutput").ap()

    es = contextlib.ExitStack()
    sb = lambda name, shape, dt=BF16: es.enter_context(nc.sbuf_tensor(name, shape, dt))
    psum = lambda name, shape: es.enter_context(nc.psum_tensor(name, shape, F32))

    xT = [sb(f"xT{c}", [128, T], F32) for c in range(NC)]
    xh = [sb(f"xh{c}", [128, T]) for c in range(NC)]
    qT = [sb(f"qT{c}", [128, T]) for c in range(NC)]
    kT = [sb(f"kT{c}", [128, T]) for c in range(4)]
    vb = [sb(f"vb{c}", [128, 8 * 65]) for c in range(16)]
    wq = [sb(f"wq{c}", [128, 1024]) for c in range(NC)]
    wk = [sb(f"wk{c}", [128, 512]) for c in range(NC)]
    wv = [sb(f"wv{c}", [128, 512]) for c in range(NC)]
    wo = wq
    bo = sb("bo", [128, 8], F32)
    Ltile = sb("Ltile", [128, 512])
    band = {off: sb(f"band{off}", [128, 512]) for off in range(-384, 257, 128)}
    ones1 = sb("ones1", [1, 128])
    onesK = sb("onesK", [128, 1])
    sq_t = sb("sq_t", [128, 512])
    stat2 = sb("stat2", [1, 512], F32)
    arow = sb("arow", [1, 512])
    brow = sb("brow", [1, 512])
    onesKf = sb("onesKf", [128, 1], F32)
    cbt = sb("cbts", [128, 352], F32)
    pt = sb("pt", [128, 512])
    rrow = stat2

    ps1 = psum("ps1", [1, 512])
    ps2 = psum("ps2", [1, 512])
    pA = psum("pA", [128, 512])
    pB = psum("pB", [128, 512])
    pC = psum("pC", [128, 512])
    pD = psum("pD", [128, 512])
    pO = psum("pO", [65, 512])
    psE = psum("psE", [128, 512])

    sc = Sched(nc, es)

    for c in range(NC):
        sc.do("sync", lambda s, c=c: s.dma_start(out=xT[c][:], in_=xTd[128 * c:128 * (c + 1), :]), "ld", 16)
    for c in range(NC):
        sc.do("gpsimd", lambda g, c=c: g.dma_start(out=wq[c][:], in_=wqd[128 * c:128 * (c + 1), :]), "ldw", 16)
        sc.do("gpsimd", lambda g, c=c: g.dma_start(out=wk[c][:], in_=wkd[128 * c:128 * (c + 1), :]), "ldw", 16)
        sc.do("gpsimd", lambda g, c=c: g.dma_start(out=wv[c][:], in_=wvd[128 * c:128 * (c + 1), :]), "ldw", 16)
    sc.do("sync", lambda s: s.dma_start(out=bo[:], in_=bod[:, :]), "ld", 16)
    sc.do("sync", lambda s: s.dma_start(out=cbt[:], in_=cbd[:, :]), "ld", 16)
    sc.do("gpsimd", lambda g: g.memset(ones1[:], 1.0))
    sc.do("gpsimd", lambda g: g.memset(onesK[:], 1.0))
    sc.do("gpsimd", lambda g: g.memset(onesKf[:], 1.0))
    for tc in range(16):
        sc.do("gpsimd", lambda g, tc=tc: g.memset(vb[tc][:], 1.0))
    sc.do("gpsimd", lambda g: g.iota(Ltile[:], pattern=[[1, 512]], base=0, channel_multiplier=-1,
                                     allow_small_or_imprecise_dtypes=True), "gp")
    for off in range(-384, 257, 128):
        sc.do("gpsimd", lambda g, off=off: g.iota(band[off][:], pattern=[[1, 512]], base=off,
                                                  channel_multiplier=-1,
                                                  allow_small_or_imprecise_dtypes=True), "gp")
    sc.wait("scalar", "gp")
    for off in range(-384, 257, 128):
        sc.do("scalar", lambda s, off=off: s.activation(band[off][:], band[off][:], AF.Abs), "abs")
    sc.do("gpsimd", lambda g: g.nop(), "gp0")
    sc.emit("gpsimd", lambda g: None)

    sc.wait("scalar", "ld", 16 * 10)
    for tt in range(NT4):
        sl = slice(512 * tt, 512 * (tt + 1))
        for c in range(NC):
            sc.do("scalar", lambda s, c=c, sl=sl: s.activation(sq_t[:], xT[c][:, sl], AF.Square), "s_sq")
            sc.wait("tensor", "s_sq")
            if tt == 0 and c == 0:
                sc.wait("tensor", "gp0")
                sc.wait("tensor", "ld", 16 * 10)
            sc.do("tensor", lambda t, c=c, sl=sl: t.matmul(ps1[:, :], onesKf[:], xT[c][:, sl],
                                                           start=(c == 0), stop=(c == NC - 1)))
            sc.do("tensor", lambda t, c=c: t.matmul(ps2[:, :], onesK[:], sq_t[:],
                                                    start=(c == 0), stop=(c == NC - 1)), "mm_st")
            sc.wait("scalar", "mm_st")
        sc.do("scalar", lambda s: s.mul(ps1[0:1, :], ps1[0:1, :], 1.0 / 1024), "s_mu")
        sc.wait("scalar", "s_mu")
        sc.do("scalar", lambda s: s.activation(psE[0:1, :], ps1[0:1, :], AF.Square), "s_m2")
        sc.wait("vector", "mm_st")
        sc.do("vector", lambda v: v.tensor_copy(stat2[:, :], ps2[:, :]), "ev_cp")
        sc.wait("vector", "s_m2")
        sc.do("vector", lambda v: v.scalar_tensor_tensor(stat2[:, :], stat2[:, :], 1.0 / 1024, psE[0:1, :],
                                                         ALU.mult, ALU.subtract), "ev_var")
        sc.wait("scalar", "ev_var")
        _ie = cidx(1e-5)
        sc.do("scalar", lambda s, _ie=_ie: s.activation(stat2[:, :], stat2[:, :], AF.Ln,
                                                        bias=cbt[0:1, _ie:_ie + 1]), "s_ln")
        sc.wait("scalar", "s_ln")
        sc.do("scalar", lambda s: s.activation(ps2[0:1, :], stat2[:, :], AF.Exp, scale=-0.5), "s_rr")
        sc.wait("scalar", "s_rr")
        sc.do("scalar", lambda s: s.activation(arow[:, :], ps2[0:1, :], AF.Copy), "s_ar")
        sc.wait("vector", "s_ar")
        sc.do("vector", lambda v: v.tensor_scalar(brow[:, :], ps1[0:1, :], -1.0, None, ALU.mult), "ev_ab")
        sc.wait("tensor", "ev_ab")
        sc.do("tensor", lambda t: t.matmul(pA[:, :], ones1[:], brow[:, :], start=True, stop=True))
        sc.do("tensor", lambda t: t.matmul(pB[:, :], ones1[:], arow[:, :], start=True, stop=True), "mm_ab")
        sc.wait("vector", "mm_ab")
        for c in range(NC):
            sc.do("vector", lambda v, c=c, sl=sl: v.tensor_add(sq_t[:, :], xT[c][:, sl], pA[:, :]))
            sc.do("vector", lambda v, c=c, sl=sl: v.tensor_mul(xh[c][:, sl], sq_t[:, :], pB[:, :]),
                  "ev_ap" if c == NC - 1 else None)
        sc.wait("tensor", "ev_ap")
        sc.wait("scalar", "ev_ap")

    sc.wait("tensor", "ldw", 16 * 24)
    sc.wait("tensor", "ev_ap")
    pq = [pC, pD]
    nproj = 0
    for kind, wlist, outlist, ncols, scale in (("q", wq, qT, 8, 0.125), ("k", wk, kT, 4, 1.0)):
        for eo in range(ncols):
            for tt in range(NT4):
                p = pq[nproj % 2]
                for c in range(NC):
                    sc.do("tensor", lambda t, p=p, c=c, eo=eo, tt=tt, wlist=wlist:
                          t.matmul(p[:, :], wlist[c][:, 128 * eo:128 * (eo + 1)],
                                   xh[c][:, 512 * tt:512 * (tt + 1)], start=(c == 0), stop=(c == NC - 1)),
                          "mm_pj" if c == NC - 1 else None)
                sc.wait("vector", "mm_pj")
                sc.do("vector", lambda v, p=p, eo=eo, tt=tt, outlist=outlist, scale=scale:
                      v.tensor_scalar(outlist[eo][:, 512 * tt:512 * (tt + 1)], p[:, :], scale, None, ALU.mult),
                      "ev_pj")
                nproj += 1
                if nproj >= 2:
                    sc.wait("tensor", "ev_pj", sc.val("ev_pj") - 1)
    sc.wait("vector", "gp0")
    for tc in range(16):
        p = pq[nproj % 2]
        for c in range(NC):
            sc.do("tensor", lambda t, p=p, c=c, tc=tc:
                  t.matmul(p[:, :], xh[c][:, 128 * tc:128 * (tc + 1)], wv[c][:, 0:512],
                           start=(c == 0), stop=(c == NC - 1)), "mm_pj" if c == NC - 1 else None)
        sc.wait("vector", "mm_pj")
        sc.do("vector", lambda v, p=p, tc=tc:
              v.tensor_copy(vb[tc][:].rearrange("p (j q) -> p j q", j=8)[:, :, 0:64],
                            p[:, :].rearrange("p (j q) -> p j q", j=8)),
              "ev_pj")
        nproj += 1
        sc.wait("tensor", "ev_pj", sc.val("ev_pj") - 1)

    sc.wait("gpsimd", "mm_pj")
    for c in range(NC):
        sc.do("gpsimd", lambda g, c=c: g.dma_start(out=wo[c][:], in_=wod[128 * c:128 * (c + 1), :]), "ldw2", 16)

    sc.wait("vector", "abs")
    sc.wait("vector", "gp")
    sc.wait("vector", "ld")
    oT = xh
    pS = [pA, pB]
    ns = 0
    for h in range(16):
        alpha = (2.0 ** (-(h + 1))) / 8.0
        j = h // 2
        kc, kb = j // 2, 64 * (j % 2)
        qc, qb_ = QPOS[h]
        for qb in range(NT4):
            qsl = slice(512 * qb, 512 * (qb + 1))
            for kt in range(16):
                p = pS[ns % 2]
                off = 512 * qb - 128 * kt
                sc.do("tensor", lambda t, p=p, kc=kc, kb=kb, kt=kt, qc=qc, qb_=qb_, qsl=qsl:
                      t.matmul(p[:, :], kT[kc][kb:kb + 64, 128 * kt:128 * (kt + 1)],
                               qT[qc][qb_:qb_ + 64, qsl], start=True, stop=True), "mm_s")
                if -384 <= off <= 256:
                    dt_, scl, cb = band[off], -alpha, 0.0
                elif off > 0:
                    dt_, scl, cb = Ltile, -alpha, -alpha * off
                else:
                    dt_, scl, cb = Ltile, alpha, alpha * off
                sc.wait("vector", "mm_s")
                sc.do("vector", lambda v, p=p, dt_=dt_, scl=scl:
                      v.scalar_tensor_tensor(psE[:, :], dt_[:, :], scl, p[:, :], ALU.mult, ALU.add),
                      "ev_s")
                sc.wait("scalar", "ev_s")
                _ic = cidx(cb - 20.0)
                sc.do("scalar", lambda s, _ic=_ic: s.activation(pt[:, :], psE[:, :], AF.Exp,
                                                                bias=cbt[:, _ic:_ic + 1]), "s_e")
                sc.wait("tensor", "s_e")
                sc.do("tensor", lambda t, kt=kt, j=j: t.matmul(pO[:, :], vb[kt][:, 65 * j:65 * j + 65],
                                                               pt[:, :], start=(kt == 0), stop=(kt == 15)),
                      "mm_o")
                ns += 1
                sc.wait("vector", "s_e")
                sc.wait("scalar", "mm_o")
            sc.wait("vector", "mm_o")
            sc.do("vector", lambda v: v.reciprocal(rrow[:, :], pO[64:65, :]), "ev_r")
            sc.wait("scalar", "ev_r")
            sc.do("scalar", lambda s: s.activation(rrow[:, :], rrow[:, :], AF.Copy), "s_rr")
            sc.do("scalar", lambda s: s.activation(pt[0:1, :], rrow[:, :], AF.Copy), "s_rb")
            sc.wait("tensor", "s_rb")
            sc.do("tensor", lambda t: t.matmul(pD[0:64, :], ones1[:, 0:64], pt[0:1, :],
                                               start=True, stop=True), "mm_rb")
            sc.wait("vector", "mm_rb")
            sc.do("vector", lambda v: v.tensor_copy(pt[64:128, 0:512], pD[0:64, :]), "ev_rb")
            sc.wait("vector", "ev_rb")
            sc.do("vector", lambda v, qc=qc, qb_=qb_, qsl=qsl:
                  v.tensor_mul(oT[qc][qb_:qb_ + 64, qsl], pO[0:64, :], pt[64:128, 0:512]), "ev_on")
            sc.wait("tensor", "ev_on")
            sc.wait("tensor", "ev_rb")

    po = [pC, pD]
    no = 0
    sc.wait("tensor", "ldw2")
    for eo in range(NC):
        for tt in range(NT4):
            p = po[no % 2]
            sl = slice(512 * tt, 512 * (tt + 1))
            for c in range(NC):
                sc.do("tensor", lambda t, p=p, c=c, eo=eo, sl=sl:
                      t.matmul(p[:, :], wo[c][:, 128 * eo:128 * (eo + 1)], oT[c][:, sl],
                               start=(c == 0), stop=(c == NC - 1)), "mm_y" if c == NC - 1 else None)
            sc.wait("vector", "mm_y")
            sc.do("vector", lambda v, p=p, eo=eo, sl=sl:
                  v.scalar_tensor_tensor(xT[eo][:, sl], p[:, :], bo[:, eo:eo + 1], xT[eo][:, sl],
                                         ALU.add, ALU.add), "ev_y")
            no += 1
            if no >= 2:
                sc.wait("tensor", "ev_y", sc.val("ev_y") - 1)
    sc.wait("sync", "ev_y")
    for c in range(NC):
        sc.do("sync", lambda s, c=c: s.dma_start(out=yTd[128 * c:128 * (c + 1), :], in_=xT[c][:]), "st", 16)
    sc.wait("sync", "st")

    with es:
        nc = sc.build()
    return nc



_CACHE = {}


def _get(name, fn):
    if name not in _CACHE:
        _CACHE[name] = fn()
    return _CACHE[name]


def run_attn_a(x, inp):
    nc = _get("attn_a", build_attn_a)
    wq = np.ascontiguousarray(np.asarray(inp["L0_attn_a_Wq"], np.float32).T[:, QPERM])
    wk = np.ascontiguousarray(np.asarray(inp["L0_attn_a_Wk"], np.float32).T)
    wv = np.ascontiguousarray(np.asarray(inp["L0_attn_a_Wv"], np.float32).T)
    wo = np.ascontiguousarray(np.asarray(inp["L0_attn_a_Wo"], np.float32).T[QPERM, :])
    bo2 = np.ascontiguousarray(np.asarray(inp["L0_attn_a_bo"], np.float32).reshape(8, 128).T)
    ctab = np.zeros((128, 352), np.float32)
    cv = np.asarray(ATTN_A_CONSTS, np.float32)
    ctab[:, :len(cv)] = cv[None, :]
    in_maps = []
    for b in range(8):
        xT = np.ascontiguousarray(x[b].T)
        in_maps.append({"xT": xT, "wqT": wq, "wkT": wk, "wvT": wv, "woT": wo, "bo2": bo2, "cbt": ctab})
    res = run_bass_kernel_spmd(nc, in_maps, core_ids=list(range(8)))
    out = np.stack([res.results[b]["yT"].T for b in range(8)], 0)
    return out.astype(np.float32)


def kernel(**inp):
    x = np.asarray(inp["L0_attn_a_x"], np.float32)
    x = run_attn_a(x, inp)
    x = run_ffn0(x, inp)
    x = run_attn_b(x, inp)
    x = run_ffn0(x, inp)
    x = run_attn1("a1", x, inp)
    x = run_ffn1(x, inp)
    x = run_attn1("b1", x, inp)
    x = run_ffn1(x, inp)
    return x


ATTN1_CONSTS = {"a1": [], "b1": []}


def rope_segs(h):
    return [(h // 4, 32 * (h % 4), 32),
            (2 + h // 4, 32 * (h % 4), 32),
            (4 + h // 2, 64 * (h % 2), 64)]


def build_attn1(kind, stop_after="full"):
    consts = ATTN1_CONSTS[kind]
    consts.clear()
    cmap = {}

    def cidx(v):
        v = float(np.float32(v))
        if v not in cmap:
            cmap[v] = len(consts)
            consts.append(v)
        return cmap[v]

    HD = 64 if kind == "a1" else 128
    NH = 16 if kind == "a1" else 8
    qscale = 1.0 / 8.0 if kind == "a1" else 1.0 / 128.0

    nc = bass.Bass("TRN2", target_bir_lowering=False)
    xTd = nc.dram_tensor("xT", [D, T], F32, kind="ExternalInput").ap()
    wqd = nc.dram_tensor("wqT", [D, 1024], F32, kind="ExternalInput").ap()
    wkd = nc.dram_tensor("wkT", [D, 1024], F32, kind="ExternalInput").ap()
    wvd = nc.dram_tensor("wvT", [D, 1024], F32, kind="ExternalInput").ap()
    wod = nc.dram_tensor("woT", [D, 1024], F32, kind="ExternalInput").ap()
    cbd = nc.dram_tensor("cbt", [128, 32], F32, kind="ExternalInput").ap()
    if kind == "b1":
        csd = nc.dram_tensor("costab", [128, T], F32, kind="ExternalInput").ap()
        snd = nc.dram_tensor("sintab", [128, T], F32, kind="ExternalInput").ap()
    yTd = nc.dram_tensor("yT", [D, T], F32, kind="ExternalOutput").ap()

    es = contextlib.ExitStack()
    sb = lambda name, shape, dt=BF16: es.enter_context(nc.sbuf_tensor(name, shape, dt))
    psum = lambda name, shape: es.enter_context(nc.psum_tensor(name, shape, F32))

    xh = [sb(f"xh{c}", [128, T]) for c in range(NC)]
    qT = [sb(f"qT{c}", [128, T]) for c in range(NC)]
    kT = [sb(f"kT{c}", [128, T]) for c in range(NC)]
    VW = NH * (HD + 1) if kind == "a1" else NH * HD
    vb = [sb(f"vb{c}", [128, VW]) for c in range(16)]
    wbuf = [sb(f"w{c}", [128, 1024]) for c in range(NC)]
    xs = [sb(f"xs{i}", [128, 512], F32) for i in range(2)]
    cbt = sb("cbts", [128, 32], F32)
    mask = {off: sb(f"mask{off}", [128, 512]) for off in range(-384, 1, 128)}
    ones1 = sb("ones1", [1, 128])
    onesK = sb("onesK", [128, 1])
    onesKf = sb("onesKf", [128, 1], F32)
    sq_t = sb("sq_t", [128, 512])
    stat2 = sb("stat2", [1, 512], F32)
    arow = sb("arow", [1, 512])
    brow = sb("brow", [1, 512])
    pt = sb("pt", [128, 512])
    yt = sb("yt", [128, 512], F32)
    if kind == "b1":
        costab = sb("costab_s", [128, T], F32)
        sintab = sb("sintab_s", [128, T], F32)
        rp1 = sb("rp1", [128, T])
        rp2 = sb("rp2", [128, T])

    ps1 = psum("ps1", [1, 512])
    ps2 = psum("ps2", [1, 512])
    pA = psum("pA", [128, 512])
    pB = psum("pB", [128, 512])
    pC = psum("pC", [128, 512])
    pD = psum("pD", [128, 512])
    pO = psum("pO", [NH and (65 if kind == "a1" else 128), 512])
    psE = psum("psE", [128, 512])

    sc = Sched(nc, es)

    sc.do("sync", lambda s: s.dma_start(out=cbt[:], in_=cbd[:, :]), "ld", 16)
    if kind == "b1":
        sc.do("sync", lambda s: s.dma_start(out=costab[:], in_=csd[:, :]), "ld", 16)
        sc.do("sync", lambda s: s.dma_start(out=sintab[:], in_=snd[:, :]), "ld", 16)
    sc.do("gpsimd", lambda g: g.memset(ones1[:], 1.0))
    sc.do("gpsimd", lambda g: g.memset(onesK[:], 1.0))
    sc.do("gpsimd", lambda g: g.memset(onesKf[:], 1.0))
    for tc in range(16):
        sc.do("gpsimd", lambda g, tc=tc: g.memset(vb[tc][:], 1.0))
    for off in range(-384, 1, 128):
        sc.do("gpsimd", lambda g, off=off: g.memset(mask[off][:], 0.0))
        sc.do("gpsimd", lambda g, off=off: g.affine_select(
            mask[off][:], mask[off][:], pattern=[[1, 512]], compare_op=ALU.is_ge,
            fill=-1e30, base=off, channel_multiplier=-1), "gp")
    sc.do("gpsimd", lambda g: g.nop(), "gp0")

    for tt in range(NT4):
        sl = slice(512 * tt, 512 * (tt + 1))
        if tt > 0:
            sc.wait("sync", "ev_ap")
        for c in range(NC):
            xst = xs[c % 2]
            if tt > 0 or c >= 2:
                sc.wait("sync", "use_x", sc.val("use_x") - 1)
            sc.do("sync", lambda s, xst=xst, c=c, sl=sl:
                  s.dma_start(out=xst[:], in_=xTd[128 * c:128 * (c + 1), sl]), "ldx", 16)
            sc.wait("scalar", "ldx")
            sc.do("scalar", lambda s, xst=xst: s.activation(sq_t[:], xst[:], AF.Square), "s_sq")
            sc.wait("tensor", "s_sq")
            if tt == 0 and c == 0:
                sc.wait("tensor", "gp0")
            sc.do("tensor", lambda t, xst=xst, c=c: t.matmul(ps1[:, :], onesKf[:], xst[:],
                                                             start=(c == 0), stop=(c == NC - 1)))
            sc.do("tensor", lambda t, c=c: t.matmul(ps2[:, :], onesK[:], sq_t[:],
                                                    start=(c == 0), stop=(c == NC - 1)), "mm_st")
            sc.do("tensor", lambda t: t.nop(), "use_x")
            sc.wait("scalar", "mm_st")
        sc.do("scalar", lambda s: s.mul(ps1[0:1, :], ps1[0:1, :], 1.0 / 1024), "s_mu")
        sc.wait("scalar", "s_mu")
        sc.do("scalar", lambda s: s.activation(psE[0:1, :], ps1[0:1, :], AF.Square), "s_m2")
        sc.wait("vector", "mm_st")
        sc.do("vector", lambda v: v.tensor_copy(stat2[:, :], ps2[:, :]), "ev_cp")
        sc.wait("vector", "s_m2")
        sc.do("vector", lambda v: v.scalar_tensor_tensor(stat2[:, :], stat2[:, :], 1.0 / 1024, psE[0:1, :],
                                                         ALU.mult, ALU.subtract), "ev_var")
        sc.wait("scalar", "ev_var")
        _ie = cidx(1e-5)
        sc.do("scalar", lambda s, _ie=_ie: s.activation(stat2[:, :], stat2[:, :], AF.Ln,
                                                        bias=cbt[0:1, _ie:_ie + 1]), "s_ln")
        sc.wait("scalar", "s_ln")
        sc.do("scalar", lambda s: s.activation(ps2[0:1, :], stat2[:, :], AF.Exp, scale=-0.5), "s_rr")
        sc.wait("scalar", "s_rr")
        sc.do("scalar", lambda s: s.activation(arow[:, :], ps2[0:1, :], AF.Copy), "s_ar")
        sc.wait("vector", "s_ar")
        sc.do("vector", lambda v: v.tensor_scalar(brow[:, :], ps1[0:1, :], -1.0, None, ALU.mult), "ev_ab")
        sc.wait("tensor", "ev_ab")
        sc.do("tensor", lambda t: t.matmul(pA[:, :], ones1[:], brow[:, :], start=True, stop=True))
        sc.do("tensor", lambda t: t.matmul(pB[:, :], ones1[:], arow[:, :], start=True, stop=True), "mm_ab")
        for c in range(NC):
            xst = xs[c % 2]
            sc.wait("sync", "use_x2", sc.val("use_x2") - 1)
            sc.wait("sync", "mm_st")
            sc.do("sync", lambda s, xst=xst, c=c, sl=sl:
                  s.dma_start(out=xst[:], in_=xTd[128 * c:128 * (c + 1), sl]), "ldx2", 16)
            sc.wait("vector", "ldx2")
            if c == 0:
                sc.wait("vector", "mm_ab")
            sc.do("vector", lambda v, xst=xst: v.tensor_add(sq_t[:, :], xst[:], pA[:, :]))
            sc.do("vector", lambda v, c=c, sl=sl: v.tensor_mul(xh[c][:, sl], sq_t[:, :], pB[:, :]), "ev_ap")
            sc.do("vector", lambda v: v.nop(), "use_x2")
        sc.wait("tensor", "ev_ap")
        sc.wait("scalar", "ev_ap")

    def load_w(src_ap, gate_sem, gate_val):
        sc.wait("gpsimd", gate_sem, gate_val)
        for c in range(NC):
            sc.do("gpsimd", lambda g, c=c: g.dma_start(out=wbuf[c][:], in_=src_ap[128 * c:128 * (c + 1), :]),
                  "ldw", 16)

    def finish_early():
        sc.wait("gpsimd", "ev_ap")
        for c in range(NC):
            sc.do("gpsimd", lambda g, c=c: g.dma_start(out=yTd[128 * c:128 * (c + 1), :], in_=xh[c][:]), "st", 16)
        sc.wait("gpsimd", "st")
        with es:
            return sc.build()

    if stop_after == "ln":
        return finish_early()

    pq = [pC, pD]
    nproj = [0]

    def proj_fm(outlist, scale, gate):
        sc.wait("tensor", "ldw", gate)
        for eo in range(NC):
            for tt in range(NT4):
                p = pq[nproj[0] % 2]
                for c in range(NC):
                    sc.do("tensor", lambda t, p=p, c=c, eo=eo, tt=tt:
                          t.matmul(p[:, :], wbuf[c][:, 128 * eo:128 * (eo + 1)],
                                   xh[c][:, 512 * tt:512 * (tt + 1)], start=(c == 0), stop=(c == NC - 1)),
                          "mm_pj" if c == NC - 1 else None)
                sc.wait("vector", "mm_pj")
                sc.do("vector", lambda v, p=p, eo=eo, tt=tt, outlist=outlist, scale=scale:
                      v.tensor_scalar(outlist[eo][:, 512 * tt:512 * (tt + 1)], p[:, :], scale, None, ALU.mult),
                      "ev_pj")
                nproj[0] += 1
                if nproj[0] >= 2:
                    sc.wait("tensor", "ev_pj", sc.val("ev_pj") - 1)

    load_w(wqd, "gp0", None)
    proj_fm(qT, qscale, 16 * 8)
    load_w(wkd, "mm_pj", sc.val("mm_pj"))
    proj_fm(kT, 1.0, 16 * 16)
    load_w(wvd, "mm_pj", sc.val("mm_pj"))
    sc.wait("tensor", "ldw", 16 * 24)
    sc.wait("vector", "gp0")
    for tc in range(16):
        for half in range(2):
            p = pq[nproj[0] % 2]
            for c in range(NC):
                sc.do("tensor", lambda t, p=p, c=c, tc=tc, half=half:
                      t.matmul(p[:, :], xh[c][:, 128 * tc:128 * (tc + 1)],
                               wbuf[c][:, 512 * half:512 * (half + 1)],
                               start=(c == 0), stop=(c == NC - 1)), "mm_pj" if c == NC - 1 else None)
            sc.wait("vector", "mm_pj")
            if kind == "a1":
                def ev(v, p=p, tc=tc, half=half):
                    dst = vb[tc][:].rearrange("p (j q) -> p j q", j=NH)[:, 8 * half:8 * half + 8, 0:HD]
                    return v.tensor_copy(dst, p[:, :].rearrange("p (j q) -> p j q", j=8))
            else:
                def ev(v, p=p, tc=tc, half=half):
                    return v.tensor_copy(vb[tc][:, 512 * half:512 * (half + 1)], p[:, :])
            sc.do("vector", ev, "ev_pj")
            nproj[0] += 1
            sc.wait("tensor", "ev_pj", sc.val("ev_pj") - 1)

    if stop_after == "proj":
        sc.wait("gpsimd", "ev_pj")
        for c in range(NC):
            sc.do("gpsimd", lambda g, c=c: g.dma_start(out=yTd[128 * c:128 * (c + 1), :], in_=qT[c][:]), "st", 16)
        sc.wait("gpsimd", "st")
        with es:
            return sc.build()

    if kind == "b1":
        sc.wait("vector", "ld", 16 * 3)
        for tens in (qT, kT):
            for g in range(2):
                ev, od = tens[g], tens[2 + g]
                sc.do("vector", lambda v, ev=ev: v.tensor_mul(rp1[:, :], ev[:], sintab[:]))
                sc.do("vector", lambda v, od=od: v.tensor_mul(rp2[:, :], od[:], costab[:]))
                sc.do("vector", lambda v, ev=ev: v.tensor_mul(ev[:], ev[:], costab[:]))
                sc.do("vector", lambda v, od=od: v.tensor_mul(od[:], od[:], sintab[:]))
                sc.do("vector", lambda v, ev=ev, od=od: v.tensor_sub(ev[:], ev[:], od[:]))
                sc.do("vector", lambda v, od=od: v.tensor_add(od[:], rp1[:, :], rp2[:, :]),
                      "ev_rope")

    oT = xh
    pS = [pA, pB]
    ns = 0
    if kind == "b1":
        sc.wait("tensor", "ev_rope")
    _i20 = cidx(-20.0)
    for h in range(NH):
        if kind == "a1":
            qc, qb_ = QPOS[h]
        else:
            qc, qb_ = h, 0
        for qb in range(NT4):
            qsl = slice(512 * qb, 512 * (qb + 1))
            kts = [kt for kt in range(16) if 512 * qb - 128 * kt >= -384]
            for ki, kt in enumerate(kts):
                p = pS[ns % 2]
                off = 512 * qb - 128 * kt
                if kind == "a1":
                    sc.do("tensor", lambda t, p=p, kt=kt, qc=qc, qb_=qb_, qsl=qsl:
                          t.matmul(p[:, :], kT[qc][qb_:qb_ + HD, 128 * kt:128 * (kt + 1)],
                                   qT[qc][qb_:qb_ + HD, qsl], start=True, stop=True), "mm_s")
                else:
                    segs = rope_segs(h)
                    for si, (c_, b_, kk) in enumerate(segs):
                        tp = (b_, 0) if b_ == 96 else None
                        sc.do("tensor", lambda t, p=p, kt=kt, c_=c_, b_=b_, kk=kk, si=si, qsl=qsl, tp=tp:
                              t.matmul(p[:, :], kT[c_][b_:b_ + kk, 128 * kt:128 * (kt + 1)],
                                       qT[c_][b_:b_ + kk, qsl], start=(si == 0), stop=(si == 2),
                                       tile_position=tp),
                              "mm_s" if si == 2 else None)
                if off <= 0:
                    sc.wait("vector", "mm_s")
                    sc.do("vector", lambda v, p=p, off=off:
                          v.tensor_add(psE[:, :], mask[off][:, :], p[:, :]), "ev_s")
                    sc.wait("scalar", "ev_s")
                    sc.do("scalar", lambda s: s.activation(pt[:, :], psE[:, :], AF.Exp,
                                                           bias=cbt[:, _i20:_i20 + 1]), "s_e")
                else:
                    sc.wait("scalar", "mm_s")
                    sc.do("scalar", lambda s, p=p: s.activation(pt[:, :], p[:, :], AF.Exp,
                                                                bias=cbt[:, _i20:_i20 + 1]), "s_e")
                sc.wait("tensor", "s_e")
                if kind == "a1":
                    j = h
                    sc.do("tensor", lambda t, kt=kt, j=j, ki=ki, last=(ki == len(kts) - 1):
                          t.matmul(pO[:, :], vb[kt][:, 65 * j:65 * j + 65], pt[:, :],
                                   start=(ki == 0), stop=last), "mm_o")
                else:
                    sc.do("tensor", lambda t, kt=kt, h=h, ki=ki, last=(ki == len(kts) - 1):
                          t.matmul(pO[:, :], vb[kt][:, 128 * h:128 * (h + 1)], pt[:, :],
                                   start=(ki == 0), stop=last), "mm_o")
                    sc.do("tensor", lambda t, ki=ki, last=(ki == len(kts) - 1):
                          t.matmul(ps1[:, :], onesK[:], pt[:, :],
                                   start=(ki == 0), stop=last), "mm_o")
                ns += 1
                sc.wait("vector", "s_e")
                sc.wait("scalar", "mm_o")
            sc.wait("vector", "mm_o")
            lsrc = pO[HD:HD + 1, :] if kind == "a1" else ps1[0:1, :]
            sc.do("vector", lambda v, lsrc=lsrc: v.reciprocal(stat2[:, :], lsrc), "ev_r")
            sc.wait("scalar", "ev_r")
            sc.do("scalar", lambda s: s.activation(pt[0:1, :], stat2[:, :], AF.Copy), "s_rb")
            sc.wait("tensor", "s_rb")
            M = HD
            sc.do("tensor", lambda t, M=M: t.matmul(pD[0:M, :], ones1[:, 0:M], pt[0:1, :],
                                                    start=True, stop=True), "mm_rb")
            sc.wait("vector", "mm_rb")
            if kind == "a1":
                sc.do("vector", lambda v: v.tensor_copy(pt[64:128, 0:512], pD[0:64, :]), "ev_rb")
                sc.wait("vector", "ev_rb")
                sc.do("vector", lambda v, qc=qc, qb_=qb_, qsl=qsl:
                      v.tensor_mul(oT[qc][qb_:qb_ + 64, qsl], pO[0:64, :], pt[64:128, 0:512]), "ev_on")
            else:
                sc.do("vector", lambda v: v.tensor_copy(pt[:, :], pD[:, :]), "ev_rb")
                sc.wait("vector", "ev_rb")
                sc.do("vector", lambda v, qc=qc, qsl=qsl:
                      v.tensor_mul(oT[qc][:, qsl], pO[0:128, :], pt[:, :]), "ev_on")
            sc.wait("tensor", "ev_on")
            sc.wait("tensor", "ev_rb")
            sc.wait("scalar", "ev_on")

    if stop_after == "attn":
        sc.wait("gpsimd", "ev_on")
        for c in range(NC):
            sc.do("gpsimd", lambda g, c=c: g.dma_start(out=yTd[128 * c:128 * (c + 1), :], in_=oT[c][:]), "st", 16)
        sc.wait("gpsimd", "st")
        with es:
            return sc.build()

    load_w(wod, "mm_o", sc.val("mm_o"))
    po = [pC, pD]
    no = [0]
    sc.wait("tensor", "ldw", 16 * 32)
    _eor = 1 if stop_after == "y1" else NC
    _ttr = 1 if stop_after == "y1" else NT4
    for eo in range(_eor):
        for tt in range(_ttr):
            p = po[no[0] % 2]
            sl = slice(512 * tt, 512 * (tt + 1))
            xst = xs[no[0] % 2]
            sc.wait("sync", "ev_y", max(0, sc.val("ev_y") - 1))
            sc.do("sync", lambda s, xst=xst, eo=eo, sl=sl:
                  s.dma_start(out=xst[:], in_=xTd[128 * eo:128 * (eo + 1), sl]), "ldx3", 16)
            for c in range(NC):
                sc.do("tensor", lambda t, p=p, c=c, eo=eo, sl=sl:
                      t.matmul(p[:, :], wbuf[c][:, 128 * eo:128 * (eo + 1)], oT[c][:, sl],
                               start=(c == 0), stop=(c == NC - 1)), "mm_y" if c == NC - 1 else None)
            sc.wait("vector", "mm_y")
            sc.wait("vector", "ldx3")
            sc.wait("vector", "st")
            sc.do("vector", lambda v, p=p, xst=xst: v.tensor_add(yt[:, :], p[:, :], xst[:]), "ev_y")
            sc.wait("sync", "ev_y")
            sc.do("sync", lambda s, eo=eo, sl=sl: s.dma_start(out=yTd[128 * eo:128 * (eo + 1), sl], in_=yt[:]),
                  "st", 16)
            no[0] += 1
            sc.wait("tensor", "ev_y", sc.val("ev_y") - 1)
    sc.wait("sync", "st")

    with es:
        nc = sc.build()
    return nc


def _host_attn1_weights(kind, inp):
    if kind == "a1":
        wqkv = np.asarray(inp["L1_attn_a_w_qkv"], np.float32)
        wq = np.zeros((1024, 1024), np.float32)
        wk = np.zeros((1024, 1024), np.float32)
        for h, (c, b) in QPOS.items():
            wq[:, c * 128 + b:c * 128 + b + 64] = wqkv[:, h, 0:64]
            wk[:, c * 128 + b:c * 128 + b + 64] = wqkv[:, h, 64:128]
        wv = np.ascontiguousarray(wqkv[:, :, 128:192].reshape(1024, 1024))
        wo = np.ascontiguousarray(np.asarray(inp["L1_attn_a_w_out"], np.float32).reshape(1024, 1024)[QPERM, :])
        extra = {}
    else:
        wqh = np.asarray(inp["L1_attn_b_wq"], np.float32)
        wkh = np.asarray(inp["L1_attn_b_wk"], np.float32)
        wq = np.zeros((1024, 1024), np.float32)
        wk = np.zeros((1024, 1024), np.float32)
        for h in range(8):
            (ec, eb, _), (oc, ob, _), (kc, kb2, _) = rope_segs(h)
            for W, Wh in ((wq, wqh), (wk, wkh)):
                W[:, ec * 128 + eb: ec * 128 + eb + 32] = Wh[h][:, 0:32]
                W[:, oc * 128 + ob: oc * 128 + ob + 32] = Wh[h][:, 32:64]
                W[:, kc * 128 + kb2: kc * 128 + kb2 + 64] = Wh[h][:, 64:128]
        wv = np.ascontiguousarray(np.asarray(inp["L1_attn_b_wv"], np.float32).transpose(1, 0, 2).reshape(1024, 1024))
        wo = np.ascontiguousarray(np.asarray(inp["L1_attn_b_wo"], np.float32))
        i = np.arange(32, dtype=np.float64)
        t = np.arange(T, dtype=np.float64)
        rad = t[None, :] * (10000.0 ** (-i / 32.0))[:, None]
        cs = np.tile(np.cos(rad).astype(np.float32), (4, 1))
        sn = np.tile(np.sin(rad).astype(np.float32), (4, 1))
        extra = {"costab": cs, "sintab": sn}
    return wq, wk, wv, wo, extra


def run_attn1(kind, x, inp):
    nc = _get("attn1_" + kind, lambda: build_attn1(kind))
    wq, wk, wv, wo, extra = _host_attn1_weights(kind, inp)
    cv = np.asarray(ATTN1_CONSTS[kind], np.float32)
    ctab = np.zeros((128, 32), np.float32)
    ctab[:, :len(cv)] = cv[None, :]
    in_maps = []
    for b in range(8):
        m = {"xT": np.ascontiguousarray(x[b].T), "wqT": wq, "wkT": wk, "wvT": wv, "woT": wo, "cbt": ctab}
        m.update(extra)
        in_maps.append(m)
    res = run_bass_kernel_spmd(nc, in_maps, core_ids=list(range(8)))
    return np.stack([res.results[b]["yT"].T for b in range(8)], 0).astype(np.float32)



def build_ab1():
    nc = bass.Bass("TRN2", target_bir_lowering=False)
    xTd = nc.dram_tensor("xT", [D, T], F32, kind="ExternalInput").ap()
    wind = nc.dram_tensor("winT", [D, 1024], F32, kind="ExternalInput").ap()
    wgtd = nc.dram_tensor("wgtT", [D, 1024], F32, kind="ExternalInput").ap()
    wbd = nc.dram_tensor("wbT", [D, 1024], F32, kind="ExternalInput").ap()
    wad = nc.dram_tensor("waT", [D, 1024], F32, kind="ExternalInput").ap()
    bind = nc.dram_tensor("bin2", [128, 8], F32, kind="ExternalInput").ap()
    bgd = nc.dram_tensor("bg2", [128, 8], F32, kind="ExternalInput").ap()
    cbd = nc.dram_tensor("cbt", [128, 8], F32, kind="ExternalInput").ap()
    gTd = nc.dram_tensor("gT", [D, T], BF16, kind="ExternalOutput").ap()
    cTd = nc.dram_tensor("cT", [D, T], BF16, kind="ExternalOutput").ap()

    es = contextlib.ExitStack()
    sb = lambda name, shape, dt=BF16: es.enter_context(nc.sbuf_tensor(name, shape, dt))
    psum = lambda name, shape: es.enter_context(nc.psum_tensor(name, shape, F32))

    xh = [sb(f"xh{c}", [128, T]) for c in range(NC)]
    iT = [sb(f"iT{c}", [128, T]) for c in range(NC)]
    gT = [sb(f"gT{c}", [128, T]) for c in range(NC)]
    uT = [sb(f"uT{c}", [128, T]) for c in range(NC)]
    wbuf = [sb(f"w{c}", [128, 1024]) for c in range(NC)]
    xs = [sb(f"xs{i}", [128, 512], F32) for i in range(2)]
    bin_ = sb("bin", [128, 8], F32)
    bg = sb("bg", [128, 8], F32)
    cbt = sb("cbts", [128, 8], F32)
    ones1 = sb("ones1", [1, 128])
    onesK = sb("onesK", [128, 1])
    onesKf = sb("onesKf", [128, 1], F32)
    sq_t = sb("sq_t", [128, 512])
    stat2 = sb("stat2", [1, 512], F32)
    arow = sb("arow", [1, 512])
    brow = sb("brow", [1, 512])
    yt = sb("yt", [128, 512])

    ps1 = psum("ps1", [1, 512])
    ps2 = psum("ps2", [1, 512])
    pA = psum("pA", [128, 512])
    pB = psum("pB", [128, 512])
    pC = psum("pC", [128, 512])
    pD = psum("pD", [128, 512])
    psE = psum("psE", [128, 512])

    sc = Sched(nc, es)
    cmap = {}
    CONSTS = AB1_CONSTS
    CONSTS.clear()

    def cidx(v):
        v = float(np.float32(v))
        if v not in cmap:
            cmap[v] = len(CONSTS)
            CONSTS.append(v)
        return cmap[v]

    sc.do("sync", lambda s: s.dma_start(out=bin_[:], in_=bind[:, :]), "ld", 16)
    sc.do("sync", lambda s: s.dma_start(out=bg[:], in_=bgd[:, :]), "ld", 16)
    sc.do("sync", lambda s: s.dma_start(out=cbt[:], in_=cbd[:, :]), "ld", 16)
    sc.do("gpsimd", lambda g: g.memset(ones1[:], 1.0))
    sc.do("gpsimd", lambda g: g.memset(onesK[:], 1.0))
    sc.do("gpsimd", lambda g: g.memset(onesKf[:], 1.0))
    sc.do("gpsimd", lambda g: g.nop(), "gp0")

    for tt in range(NT4):
        sl = slice(512 * tt, 512 * (tt + 1))
        if tt > 0:
            sc.wait("sync", "ev_ap")
        for c in range(NC):
            xst = xs[c % 2]
            if tt > 0 or c >= 2:
                sc.wait("sync", "use_x", sc.val("use_x") - 1)
            sc.do("sync", lambda s, xst=xst, c=c, sl=sl:
                  s.dma_start(out=xst[:], in_=xTd[128 * c:128 * (c + 1), sl]), "ldx", 16)
            sc.wait("scalar", "ldx")
            sc.do("scalar", lambda s, xst=xst: s.activation(sq_t[:], xst[:], AF.Square), "s_sq")
            sc.wait("tensor", "s_sq")
            if tt == 0 and c == 0:
                sc.wait("tensor", "gp0")
            sc.do("tensor", lambda t, xst=xst, c=c: t.matmul(ps1[:, :], onesKf[:], xst[:],
                                                             start=(c == 0), stop=(c == NC - 1)))
            sc.do("tensor", lambda t, c=c: t.matmul(ps2[:, :], onesK[:], sq_t[:],
                                                    start=(c == 0), stop=(c == NC - 1)), "mm_st")
            sc.do("tensor", lambda t: t.nop(), "use_x")
            sc.wait("scalar", "mm_st")
        sc.do("scalar", lambda s: s.mul(ps1[0:1, :], ps1[0:1, :], 1.0 / 1024), "s_mu")
        sc.wait("scalar", "s_mu")
        sc.do("scalar", lambda s: s.activation(psE[0:1, :], ps1[0:1, :], AF.Square), "s_m2")
        sc.wait("vector", "mm_st")
        sc.do("vector", lambda v: v.tensor_copy(stat2[:, :], ps2[:, :]), "ev_cp")
        sc.wait("vector", "s_m2")
        sc.do("vector", lambda v: v.scalar_tensor_tensor(stat2[:, :], stat2[:, :], 1.0 / 1024, psE[0:1, :],
                                                         ALU.mult, ALU.subtract), "ev_var")
        sc.wait("scalar", "ev_var")
        _ie = cidx(1e-5)
        sc.do("scalar", lambda s, _ie=_ie: s.activation(stat2[:, :], stat2[:, :], AF.Ln,
                                                        bias=cbt[0:1, _ie:_ie + 1]), "s_ln")
        sc.wait("scalar", "s_ln")
        sc.do("scalar", lambda s: s.activation(ps2[0:1, :], stat2[:, :], AF.Exp, scale=-0.5), "s_rr")
        sc.wait("scalar", "s_rr")
        sc.do("scalar", lambda s: s.activation(arow[:, :], ps2[0:1, :], AF.Copy), "s_ar")
        sc.wait("vector", "s_ar")
        sc.do("vector", lambda v: v.tensor_scalar(brow[:, :], ps1[0:1, :], -1.0, None, ALU.mult), "ev_ab")
        sc.wait("tensor", "ev_ab")
        sc.do("tensor", lambda t: t.matmul(pA[:, :], ones1[:], brow[:, :], start=True, stop=True))
        sc.do("tensor", lambda t: t.matmul(pB[:, :], ones1[:], arow[:, :], start=True, stop=True), "mm_ab")
        for c in range(NC):
            xst = xs[c % 2]
            sc.wait("sync", "use_x2", sc.val("use_x2") - 1)
            sc.wait("sync", "mm_st")
            sc.do("sync", lambda s, xst=xst, c=c, sl=sl:
                  s.dma_start(out=xst[:], in_=xTd[128 * c:128 * (c + 1), sl]), "ldx2", 16)
            sc.wait("vector", "ldx2")
            if c == 0:
                sc.wait("vector", "mm_ab")
            sc.do("vector", lambda v, xst=xst: v.tensor_add(sq_t[:, :], xst[:], pA[:, :]))
            sc.do("vector", lambda v, c=c, sl=sl: v.tensor_mul(xh[c][:, sl], sq_t[:, :], pB[:, :]), "ev_ap")
            sc.do("vector", lambda v: v.nop(), "use_x2")
        sc.wait("tensor", "ev_ap")
        sc.wait("scalar", "ev_ap")

    pq = [pC, pD]
    nproj = [0]

    def load_w(src_ap, gate_sem, gate_val):
        sc.wait("gpsimd", gate_sem, gate_val)
        for c in range(NC):
            sc.do("gpsimd", lambda g, c=c: g.dma_start(out=wbuf[c][:], in_=src_ap[128 * c:128 * (c + 1), :]),
                  "ldw", 16)

    def proj(srclist, gate, evac):
        sc.wait("tensor", "ldw", gate)
        for eo in range(NC):
            for tt in range(NT4):
                p = pq[nproj[0] % 2]
                for c in range(NC):
                    sc.do("tensor", lambda t, p=p, c=c, eo=eo, tt=tt:
                          t.matmul(p[:, :], wbuf[c][:, 128 * eo:128 * (eo + 1)],
                                   srclist[c][:, 512 * tt:512 * (tt + 1)], start=(c == 0), stop=(c == NC - 1)),
                          "mm_pj" if c == NC - 1 else None)
                evac(p, eo, tt)
                nproj[0] += 1
                if nproj[0] >= 2:
                    sc.wait("tensor", "ev_pj", sc.val("ev_pj") - 1)

    load_w(wind, "gp0", None)

    def ev_i(p, eo, tt):
        sc.wait("vector", "mm_pj")
        sc.do("vector", lambda v, p=p, eo=eo, tt=tt:
              v.tensor_scalar(iT[eo][:, 512 * tt:512 * (tt + 1)], p[:, :], bin_[:, eo:eo + 1],
                              None, ALU.add), "ev_pj")
    proj(xh, 16 * 8, ev_i)

    load_w(wgtd, "mm_pj", sc.val("mm_pj"))

    def ev_g(p, eo, tt):
        sc.wait("scalar", "mm_pj")
        sc.do("scalar", lambda s, p=p, eo=eo, tt=tt:
              s.activation(gT[eo][:, 512 * tt:512 * (tt + 1)], p[:, :], AF.Sigmoid,
                           bias=bg[:, eo:eo + 1]), "ev_pj")
    proj(iT, 16 * 16, ev_g)

    load_w(wbd, "mm_pj", sc.val("mm_pj"))

    def ev_u(p, eo, tt):
        sc.wait("vector", "mm_pj")
        sc.do("vector", lambda v, p=p, eo=eo, tt=tt:
              v.tensor_copy(uT[eo][:, 512 * tt:512 * (tt + 1)], p[:, :]), "ev_pj")
    proj(iT, 16 * 24, ev_u)

    sc.wait("gpsimd", "ev_pj", sc.val("ev_pj"))
    for c in range(NC):
        sc.do("gpsimd", lambda g, c=c: g.dma_start(out=gTd[128 * c:128 * (c + 1), :], in_=gT[c][:]), "st", 16)

    for c in range(NC):
        for tt in range(NT4):
            sl = slice(512 * tt, 512 * (tt + 1))
            sc.do("vector", lambda v, c=c, sl=sl: v.tensor_mul(sq_t[:, :], gT[c][:, sl], iT[c][:, sl]))
            sc.do("vector", lambda v, c=c, sl=sl: v.tensor_sub(xh[c][:, sl], iT[c][:, sl], sq_t[:, :]),
                  "ev_z" if tt == NT4 - 1 else None)

    load_w(wad, "ev_pj", sc.val("ev_pj"))
    sc.wait("tensor", "ev_z")

    def ev_c(p, eo, tt):
        sl = slice(512 * tt, 512 * (tt + 1))
        sc.wait("vector", "mm_pj")
        sc.wait("vector", "stc")
        sc.do("vector", lambda v, p=p, eo=eo, sl=sl:
              v.tensor_add(yt[:, :], p[:, :], uT[eo][:, sl]), "ev_pj")
        sc.wait("sync", "ev_pj")
        sc.do("sync", lambda s, eo=eo, sl=sl: s.dma_start(out=cTd[128 * eo:128 * (eo + 1), sl], in_=yt[:]),
              "stc", 16)
    proj(xh, 16 * 32, ev_c)
    sc.wait("sync", "stc")
    sc.wait("gpsimd", "st")

    with es:
        nc = sc.build()
    return nc


AB1_CONSTS = []
RSTEPS = 256


def build_ab2():
    nc = bass.Bass("TRN2", target_bir_lowering=False)
    Ad = nc.dram_tensor("AT", [1024, 1024], F32, kind="ExternalInput").ap()
    sind = nc.dram_tensor("sin0", [128, 64], F32, kind="ExternalInput").ap()
    gd = nc.dram_tensor("gimg", [128, 64 * RSTEPS], BF16, kind="ExternalInput").ap()
    cd = nc.dram_tensor("cimg", [128, 64 * RSTEPS], BF16, kind="ExternalInput").ap()
    std = nc.dram_tensor("simg", [128, 64 * RSTEPS], BF16, kind="ExternalOutput").ap()
    soutd = nc.dram_tensor("sout", [128, 64], F32, kind="ExternalOutput").ap()

    es = contextlib.ExitStack()
    sb = lambda name, shape, dt=BF16: es.enter_context(nc.sbuf_tensor(name, shape, dt))
    AT = [es.enter_context(nc.sbuf_tensor(f"AT{d}", [128, 1024], BF16)) for d in range(8)]
    gbuf = sb("gbuf", [128, 64 * RSTEPS])
    cbuf = sb("cbuf", [128, 64 * RSTEPS])
    stbuf = sb("stbuf", [128, 64 * RSTEPS])
    blended = sb("blended", [128, 64])
    t1 = sb("t1", [128, 64], F32)
    sprev = sb("sprev", [128, 64], F32)
    psE = [es.enter_context(nc.psum_tensor(f"psE{i}", [128, 8], F32)) for i in range(8)]
    sc = Sched(nc, es)

    for d2 in range(8):
        sc.do("gpsimd", lambda g, d2=d2: g.dma_start(out=AT[d2][:], in_=Ad[128 * d2:128 * (d2 + 1), :]), "ld", 16)
    sc.do("sync", lambda s: s.dma_start(out=gbuf[:], in_=gd[:, :]), "ld", 16)
    sc.do("sync", lambda s: s.dma_start(out=cbuf[:], in_=cd[:, :]), "ld", 16)
    sc.do("sync", lambda s: s.dma_start(out=sprev[:], in_=sind[:, :]), "ld", 16)

    gv = gbuf[:].rearrange("p (cb t) -> p t cb", t=RSTEPS)
    cv = cbuf[:].rearrange("p (cb t) -> p t cb", t=RSTEPS)
    sv = stbuf[:].rearrange("p (cb t) -> p t cb", t=RSTEPS)

    sc.wait("vector", "ld", 16 * 11)
    sc.do("vector", lambda v: v.tensor_mul(blended[:], gv[:, 0, :], sprev[:]), "dv")

    sc.wait("tensor", "ld", 16 * 11)
    for step in range(RSTEPS):
        sc.wait("tensor", "dv", step + 1)

        def mms(t, step=step):
            for ec in range(8):
                for d2 in range(8):
                    t.matmul(psE[ec][:, 0:8], AT[d2][:, 128 * ec:128 * (ec + 1)],
                             blended[:, 8 * d2:8 * (d2 + 1)], start=(d2 == 0), stop=(d2 == 7))
        sc.emit("tensor", mms)
        sc.do("tensor", lambda t: t.nop(), "pe")
        sc.wait("vector", "pe", step + 1)
        for ec in range(7):
            sc.do("vector", lambda v, ec=ec, step=step:
                  v.tensor_add(t1[:, 8 * ec:8 * (ec + 1)], psE[ec][:, 0:8], cv[:, step, 8 * ec:8 * (ec + 1)]))
        sc.do("vector", lambda v, step=step:
              v.tensor_add(t1[:, 56:64], psE[7][:, 0:8], cv[:, step, 56:64]), "ac")
        sc.wait("scalar", "ac", step + 1)
        sc.do("scalar", lambda s, step=step: s.activation(sv[:, step, :], t1[:], AF.Tanh), "th")
        if step < RSTEPS - 1:
            sc.wait("vector", "th", step + 1)
            sc.do("vector", lambda v, step=step:
                  v.tensor_mul(blended[:], gv[:, step + 1, :], sv[:, step, :]), "dv")

    sc.wait("scalar", "th", RSTEPS)
    sc.do("scalar", lambda s: s.activation(sprev[:], sv[:, RSTEPS - 1, :], AF.Copy), "so")
    sc.wait("sync", "so")
    sc.do("sync", lambda s: s.dma_start(out=soutd[:, :], in_=sprev[:]), "st", 16)
    sc.wait("sync", "th", RSTEPS)
    sc.do("sync", lambda s: s.dma_start(out=std[:, :], in_=stbuf[:]), "st", 16)
    sc.wait("sync", "st")
    with es:
        nc = sc.build()
    return nc


def build_ab3():
    nc = bass.Bass("TRN2", target_bir_lowering=False)
    sTd = nc.dram_tensor("sT", [D, T], BF16, kind="ExternalInput").ap()
    xTd = nc.dram_tensor("xT", [D, T], F32, kind="ExternalInput").ap()
    wod = nc.dram_tensor("woT", [D, 1024], F32, kind="ExternalInput").ap()
    bod = nc.dram_tensor("bo2", [128, 8], F32, kind="ExternalInput").ap()
    yTd = nc.dram_tensor("yT", [D, T], F32, kind="ExternalOutput").ap()
    es = contextlib.ExitStack()
    sb = lambda name, shape, dt=BF16: es.enter_context(nc.sbuf_tensor(name, shape, dt))
    sT = [sb(f"sT{c}", [128, T]) for c in range(NC)]
    wbuf = [sb(f"w{c}", [128, 1024]) for c in range(NC)]
    xs = [sb(f"xs{i}", [128, 512], F32) for i in range(2)]
    yt = sb("yt", [128, 512], F32)
    bo = sb("bo", [128, 8], F32)
    pC = es.enter_context(nc.psum_tensor("pC", [128, 512], F32))
    pD = es.enter_context(nc.psum_tensor("pD", [128, 512], F32))
    sc = Sched(nc, es)
    for c in range(NC):
        sc.do("sync", lambda s, c=c: s.dma_start(out=sT[c][:], in_=sTd[128 * c:128 * (c + 1), :]), "ld", 16)
        sc.do("gpsimd", lambda g, c=c: g.dma_start(out=wbuf[c][:], in_=wod[128 * c:128 * (c + 1), :]), "ldw", 16)
    sc.do("sync", lambda s: s.dma_start(out=bo[:], in_=bod[:, :]), "ld", 16)
    po = [pC, pD]
    no = [0]
    sc.wait("tensor", "ldw")
    sc.wait("tensor", "ld")
    sc.wait("vector", "ld")
    for eo in range(NC):
        for tt in range(NT4):
            p = po[no[0] % 2]
            sl = slice(512 * tt, 512 * (tt + 1))
            xst = xs[no[0] % 2]
            sc.wait("sync", "ev_y", max(0, sc.val("ev_y") - 1))
            sc.do("sync", lambda s, xst=xst, eo=eo, sl=sl:
                  s.dma_start(out=xst[:], in_=xTd[128 * eo:128 * (eo + 1), sl]), "ldx3", 16)
            for c in range(NC):
                sc.do("tensor", lambda t, p=p, c=c, eo=eo, sl=sl:
                      t.matmul(p[:, :], wbuf[c][:, 128 * eo:128 * (eo + 1)], sT[c][:, sl],
                               start=(c == 0), stop=(c == NC - 1)), "mm_y" if c == NC - 1 else None)
            sc.wait("vector", "mm_y")
            sc.wait("vector", "ldx3")
            sc.wait("vector", "st")
            sc.do("vector", lambda v, p=p, xst=xst, eo=eo:
                  v.scalar_tensor_tensor(yt[:, :], p[:, :], bo[:, eo:eo + 1], xst[:], ALU.add, ALU.add), "ev_y")
            sc.wait("sync", "ev_y")
            sc.do("sync", lambda s, eo=eo, sl=sl: s.dma_start(out=yTd[128 * eo:128 * (eo + 1), sl], in_=yt[:]),
                  "st", 16)
            no[0] += 1
            sc.wait("tensor", "ev_y", sc.val("ev_y") - 1)
    sc.wait("sync", "st")
    with es:
        nc = sc.build()
    return nc


def run_attn_b(x, inp):
    nc1 = _get("ab1", build_ab1)
    winT = np.ascontiguousarray(np.asarray(inp["L0_attn_b_W_in"], np.float32).T)
    wgtT = np.ascontiguousarray(np.asarray(inp["L0_attn_b_W_gate"], np.float32).T)
    wbT = np.ascontiguousarray(np.asarray(inp["L0_attn_b_B"], np.float32).T)
    waT = np.ascontiguousarray(np.asarray(inp["L0_attn_b_A"], np.float32).T)
    r2 = lambda b: np.ascontiguousarray(np.asarray(b, np.float32).reshape(8, 128).T)
    bin2 = r2(inp["L0_attn_b_b_in"])
    bg2 = r2(inp["L0_attn_b_b_gate"])
    cv = np.asarray(AB1_CONSTS, np.float32)
    ctab = np.zeros((128, 8), np.float32)
    ctab[:, :len(cv)] = cv[None, :]
    in_maps = [{"xT": np.ascontiguousarray(x[b].T), "winT": winT, "wgtT": wgtT, "wbT": wbT,
                "waT": waT, "bin2": bin2, "bg2": bg2, "cbt": ctab} for b in range(8)]
    res = run_bass_kernel_spmd(nc1, in_maps, core_ids=list(range(8)))
    gs = np.stack([res.results[b]["gT"] for b in range(8)], 0)
    cs = np.stack([res.results[b]["cT"] for b in range(8)], 0)

    import ml_dtypes
    def images(arr):
        a = arr.reshape(8, 8, 128, 8, RSTEPS)
        a = a.transpose(3, 2, 1, 0, 4)
        return [np.ascontiguousarray(a[s].reshape(128, 64 * RSTEPS)) for s in range(8)]
    gimg = images(gs)
    cimg = images(cs)

    nc2 = _get("ab2", build_ab2)
    AT = waT
    sin0 = np.zeros((128, 64), np.float32)
    simgs = []
    for s in range(8):
        m = {"AT": AT, "sin0": sin0, "gimg": gimg[s], "cimg": cimg[s]}
        res2 = run_bass_kernel_spmd(nc2, [m] * 8, core_ids=list(range(8)))
        simgs.append(res2.results[0]["simg"])
        sin0 = res2.results[0]["sout"]
    st = np.stack(simgs, 0).reshape(8, 128, 64, RSTEPS)
    st = st.reshape(8, 128, 8, 8, RSTEPS).transpose(3, 2, 1, 0, 4)
    states = st.reshape(8, 1024, 8 * RSTEPS)

    nc3 = _get("ab3", build_ab3)
    woT = np.ascontiguousarray(np.asarray(inp["L0_attn_b_W_out"], np.float32).T)
    bo2 = r2(inp["L0_attn_b_b_out"])
    in_maps = [{"sT": np.ascontiguousarray(states[b]), "xT": np.ascontiguousarray(x[b].T),
                "woT": woT, "bo2": bo2} for b in range(8)]
    res3 = run_bass_kernel_spmd(nc3, in_maps, core_ids=list(range(8)))
    return np.stack([res3.results[b]["yT"].T for b in range(8)], 0).astype(np.float32)


LC = 1024
FF1 = 2048


def build_ffn1(part, rcut=99):
    nc = bass.Bass("TRN2", target_bir_lowering=False)
    if part == "route":
        xTd = nc.dram_tensor("xT", [D, T], F32, kind="ExternalInput").ap()
        rwd = nc.dram_tensor("rwpad", [D, 512], F32, kind="ExternalInput").ap()
        lmd = nc.dram_tensor("lmat_in", [128, 128], F32, kind="ExternalInput").ap()
        rbd = nc.dram_tensor("rb", [1, 8], F32, kind="ExternalInput").ap()
        tmpd = nc.dram_tensor("temp", [1, 1], F32, kind="ExternalInput").ap()
        xrows_d = nc.dram_tensor("xrows", [T, D], BF16, kind="ExternalOutput").ap()
        idx_d = nc.dram_tensor("idxtab", [8 * LC, 1], I32, kind="ExternalOutput").ap()
        sld = nc.dram_tensor("slout", [128, 32], I32, kind="ExternalOutput").ap()
        dbgd = nc.dram_tensor("dbg", [128, 128], F32, kind="ExternalOutput").ap()
        dbg2d = nc.dram_tensor("dbg2", [8, T], F32, kind="ExternalOutput").ap()
        dbg3d = nc.dram_tensor("dbg3", [128, 128], F32, kind="ExternalOutput").ap()
        w12d = nc.dram_tensor("w12out", [128, 32], F32, kind="ExternalOutput").ap()
        xnd = nc.dram_tensor("xnT", [D, T], BF16, kind="ExternalOutput").ap()
    elif part in ("exp0", "exp1"):
        xrows_d = nc.dram_tensor("xrows", [T, D], BF16, kind="ExternalInput").ap()
        idx_d = nc.dram_tensor("idxtab", [8 * LC, 1], I32, kind="ExternalInput").ap()
        wgd = nc.dram_tensor("wg", [4, D, FF1], F32, kind="ExternalInput").ap()
        wpd = nc.dram_tensor("wp", [4, D, FF1], F32, kind="ExternalInput").ap()
        wod = nc.dram_tensor("wo", [4, FF1, D], F32, kind="ExternalInput").ap()
        yb_d = nc.dram_tensor("ybufp", [4 * LC, D], BF16, kind="ExternalOutput").ap()
        dbgi = nc.dram_tensor("dbgi", [128, 8], I32, kind="ExternalOutput").ap()
        dbgx = nc.dram_tensor("dbgx", [128, 2048], BF16, kind="ExternalOutput").ap()
    else:
        xTd = nc.dram_tensor("xT", [D, T], F32, kind="ExternalInput").ap()
        xnd = nc.dram_tensor("xnT", [D, T], BF16, kind="ExternalInput").ap()
        yb_d = nc.dram_tensor("ybuf", [8 * LC, D], BF16, kind="ExternalInput").ap()
        sld = nc.dram_tensor("slout", [128, 32], I32, kind="ExternalInput").ap()
        w12d = nc.dram_tensor("w12out", [128, 32], F32, kind="ExternalInput").ap()
        yTd = nc.dram_tensor("yT", [D, T], F32, kind="ExternalOutput").ap()

    es = contextlib.ExitStack()
    sb = lambda name, shape, dt=BF16: es.enter_context(nc.sbuf_tensor(name, shape, dt))
    psum = lambda name, shape, dt=F32: es.enter_context(nc.psum_tensor(name, shape, dt))

    ident = sb("ident", [128, 128])
    sl_i = sb("sl_i", [128, 32], I32)
    if part == "route":
        xh = [sb(f"xh{c}", [128, T]) for c in range(NC)]
        identf = sb("identf", [128, 128], F32)
        xs = [sb(f"xs{i}", [128, 512], F32) for i in range(2)]
        la = sb("la", [128, 128], F32)
        thB = sb("thB", [128, 128], F32)
        cnt2 = sb("cnt2", [128, 128], F32)
        cntT2 = sb("cntT2", [128, 128], F32)
        z128 = sb("z128", [128, 128], F32)
        totB = sb("totB", [128, 1])
        lmat = sb("lmat", [128, 128])
        iotaE = sb("iotaE", [128, 8], F32)
        rb128 = sb("rb128", [128, 8], F32)
        rt128 = sb("rt128", [128, 1], F32)
        m1 = sb("m1", [128, 1], F32)
        s1 = sb("s1", [128, 1], F32)
        l1 = sb("l1", [128, 1], F32)
        msk = sb("msk", [128, 8], F32)
        gnorm = sb("gnorm", [128, 8], F32)
        w1g = sb("w1g", [128, 16], F32)
        w2g = sb("w2g", [128, 16], F32)
        sl_f = sb("sl_f", [128, 32], F32)
        tokid = sb("tokid", [128, 16], I32)
        oobf = sb("oobf", [128, 64], I32)
        wpad = [sb(f"wpad{c}", [128, 512]) for c in range(NC)]
        rbb = sb("rbb", [128, 8], F32)
        xg = [sb("xg0", [128, D])]
        rrow_f = sb("rrow_f", [1, T], F32)
        murow_f = sb("murow_f", [1, T], F32)
    elif part in ("exp0", "exp1"):
        gT = [sb(f"gTf{c}", [128, LC]) for c in range(16)]
        pT = [sb(f"pTf{c}", [128, LC]) for c in range(16)]
        wbuf = [sb(f"w{c}", [128, FF1]) for c in range(NC)]
        xg = [sb(f"xg{i}", [128, D]) for i in range(8)]
        xgT = [sb(f"xgT{c}", [128, LC]) for c in range(NC)]
    else:
        xh = [sb(f"xh{c}", [128, T]) for c in range(NC)]
        xs = [sb(f"xs{i}", [128, 512], F32) for i in range(2)]
        w1g = sb("w1g", [128, 16], F32)
        w2g = sb("w2g", [128, 16], F32)
        ytok = [sb(f"ytok{i}", [128, D]) for i in range(2)]
        yt = sb("yt", [128, 512], F32)
    cb_s = sb("cbs", [128, 4], F32)
    ones1 = sb("ones1", [1, 128])
    onesK = sb("onesK", [128, 1])
    onesKf = sb("onesKf", [128, 1], F32)
    sq_t = sb("sq_t", [128, 512])
    stat2 = sb("stat2", [1, 512], F32)
    arow = sb("arow", [1, 512])
    brow = sb("brow", [1, 512])

    cbd = nc.dram_tensor("cbt", [128, 4], F32, kind="ExternalInput").ap()
    ps1 = psum("ps1", [1, 512])
    ps2 = psum("ps2", [1, 512])
    pA = psum("pA", [128, 512])
    pB = psum("pB", [128, 512])
    pTb = psum("pTb", [128, 512], BF16)
    pR = psum("pR", [128, 8])
    if part == "route":
        pJ0 = psum("pJ0", [128, 8])
        pJ1 = psum("pJ1", [128, 8])
    else:
        pC = psum("pC", [128, 512])
        pD = psum("pD", [128, 512])

    sc = Sched(nc, es)
    cmap = {}
    FFN1_CONSTS.clear()

    def cidx(v):
        v = float(np.float32(v))
        if v not in cmap:
            cmap[v] = len(FFN1_CONSTS)
            FFN1_CONSTS.append(v)
        return cmap[v]

    sc.do("sync", lambda s: s.dma_start(out=cb_s[:], in_=cbd[:, :]), "ld", 16)
    sc.do("gpsimd", lambda g: g.memset(ones1[:], 1.0))
    sc.do("gpsimd", lambda g: g.memset(onesK[:], 1.0))
    sc.do("gpsimd", lambda g: g.memset(onesKf[:], 1.0))
    sc.do("gpsimd", lambda g: g.memset(sq_t[:], 1.0), "gpm")
    sc.wait("gpsimd", "gpm")
    sc.do("gpsimd", lambda g: g.affine_select(ident[:], sq_t[:, 0:128], pattern=[[-1, 128]],
                                              compare_op=ALU.is_equal, fill=0.0, base=0,
                                              channel_multiplier=1), "gp")
    if part == "route":
        sc.do("sync", lambda s: s.dma_start(out=rb128[0:1, :], in_=rbd[:, :]), "ld", 16)
        sc.do("gpsimd", lambda g: g.dma_start(out=lmat[:], in_=lmd[:, :]), "ldlm", 16)

        sc.do("sync", lambda s: s.dma_start(out=rt128[0:1, :], in_=tmpd[:, :]), "ld", 16)


        sc.do("gpsimd", lambda g: g.memset(z128[:], 0.0))
        sc.do("gpsimd", lambda g: g.memset(oobf[:], 1 << 29))
        sc.do("gpsimd", lambda g: g.iota(iotaE[:], pattern=[[1, 8]], base=0, channel_multiplier=0,
                                         allow_small_or_imprecise_dtypes=True), "gp")
        for g_ in range(16):
            sc.do("gpsimd", lambda g, g_=g_: g.iota(tokid[:, g_:g_ + 1], pattern=[[1, 1]], base=128 * g_,
                                                    channel_multiplier=1), "gp")
        sc.do("gpsimd", lambda g: g.memset(identf[:], 0.0), "gpm")
        sc.do("gpsimd", lambda g: g.memset(la[:], 1.0), "gpm")
        sc.wait("gpsimd", "gpm")
        sc.do("gpsimd", lambda g: g.affine_select(identf[:], la[:, 0:128], pattern=[[-1, 128]],
                                                  compare_op=ALU.is_equal, fill=0.0, base=0,
                                                  channel_multiplier=1), "gp")
        sc.do("gpsimd", lambda g: g.dma_start(out=idx_d[:, :].rearrange("(a b) o -> a (b o)", a=128),
                                              in_=oobf[:, :]), "oob", 16)

    sc.do("gpsimd", lambda g: g.nop(), "gp0")

    if part == "route":
        for tt in range(NT4):
            sl = slice(512 * tt, 512 * (tt + 1))
            if tt > 0:
                sc.wait("sync", "ev_ap")
            for c in range(NC):
                xst = xs[c % 2]
                if tt > 0 or c >= 2:
                    sc.wait("sync", "use_x", sc.val("use_x") - 1)
                sc.do("sync", lambda s, xst=xst, c=c, sl=sl:
                      s.dma_start(out=xst[:], in_=xTd[128 * c:128 * (c + 1), sl]), "ldx", 16)
                sc.wait("scalar", "ldx")
                sc.do("scalar", lambda s, xst=xst: s.activation(sq_t[:], xst[:], AF.Square), "s_sq")
                sc.wait("tensor", "s_sq")
                if tt == 0 and c == 0:
                    sc.wait("tensor", "gp0")
                sc.do("tensor", lambda t, xst=xst, c=c: t.matmul(ps1[:, :], onesKf[:], xst[:],
                                                                 start=(c == 0), stop=(c == NC - 1)))
                sc.do("tensor", lambda t, c=c: t.matmul(ps2[:, :], onesK[:], sq_t[:],
                                                        start=(c == 0), stop=(c == NC - 1)), "mm_st")
                sc.do("tensor", lambda t: t.nop(), "use_x")
                sc.wait("scalar", "mm_st")
            sc.do("scalar", lambda s: s.mul(ps1[0:1, :], ps1[0:1, :], 1.0 / 1024), "s_mu")
            sc.wait("scalar", "s_mu")
            sc.do("scalar", lambda s: s.activation(pA[0:1, :], ps1[0:1, :], AF.Square), "s_m2")
            sc.wait("vector", "mm_st")
            sc.do("vector", lambda v: v.tensor_copy(stat2[:, :], ps2[:, :]), "ev_cp")
            sc.wait("vector", "s_m2")
            sc.do("vector", lambda v: v.scalar_tensor_tensor(stat2[:, :], stat2[:, :], 1.0 / 1024, pA[0:1, :],
                                                             ALU.mult, ALU.subtract), "ev_var")
            sc.wait("scalar", "ev_var")
            _ie = cidx(1e-5)
            sc.do("scalar", lambda s, _ie=_ie: s.activation(stat2[:, :], stat2[:, :], AF.Ln,
                                                            bias=cb_s[0:1, _ie:_ie + 1]), "s_ln")
            sc.wait("scalar", "s_ln")
            sc.do("scalar", lambda s: s.activation(ps2[0:1, :], stat2[:, :], AF.Exp, scale=-0.5), "s_rr")
            sc.wait("scalar", "s_rr")
            sc.do("scalar", lambda s: s.activation(arow[:, :], ps2[0:1, :], AF.Copy), "s_ar")
            sc.do("scalar", lambda s, sl=sl: s.activation(rrow_f[0:1, sl], ps2[0:1, :], AF.Copy), "s_ar")
            sc.do("scalar", lambda s, sl=sl: s.activation(murow_f[0:1, sl], ps1[0:1, :], AF.Copy), "s_ar")
            sc.wait("vector", "s_ar")
            sc.do("vector", lambda v: v.tensor_scalar(brow[:, :], ps1[0:1, :], -1.0, None, ALU.mult), "ev_ab")
            sc.wait("tensor", "ev_ab")
            sc.do("tensor", lambda t: t.matmul(pA[:, :], ones1[:], brow[:, :], start=True, stop=True))
            sc.do("tensor", lambda t: t.matmul(pB[:, :], ones1[:], arow[:, :], start=True, stop=True), "mm_ab")
            for c in range(NC):
                xst = xs[c % 2]
                sc.wait("sync", "use_x2", sc.val("use_x2") - 1)
                sc.wait("sync", "mm_st")
                sc.do("sync", lambda s, xst=xst, c=c, sl=sl:
                      s.dma_start(out=xst[:], in_=xTd[128 * c:128 * (c + 1), sl]), "ldx2", 16)
                sc.wait("vector", "ldx2")
                if c == 0:
                    sc.wait("vector", "mm_ab")
                sc.do("vector", lambda v, xst=xst: v.tensor_add(sq_t[:, :], xst[:], pA[:, :]))
                sc.do("vector", lambda v, c=c, sl=sl: v.tensor_mul(xh[c][:, sl], sq_t[:, :], pB[:, :]), "ev_ap")
                sc.do("vector", lambda v: v.nop(), "use_x2")
            sc.wait("tensor", "ev_ap")
            sc.wait("scalar", "ev_ap")

        if rcut >= 1:
            sc.wait("vector", "ld", 16 * 3)
            sc.do("vector", lambda v: v.tensor_scalar(rt128[0:1, :], rt128[0:1, :], 0.1, None, ALU.max), "ev_t")
            sc.wait("vector", "ev_t")
            sc.do("vector", lambda v: v.reciprocal(rt128[0:1, :], rt128[0:1, :]), "ev_t")
            sc.wait("scalar", "ev_t")
            sc.do("scalar", lambda s: s.activation(arow[0:1, 0:1], rt128[0:1, 0:1], AF.Copy), "s_t")
            sc.do("scalar", lambda s: s.activation(arow[0:1, 8:16], rb128[0:1, 0:8], AF.Copy), "s_t")
            sc.wait("tensor", "s_t")
            sc.do("tensor", lambda t: t.matmul(pJ0[:, 0:1], ones1[:], arow[0:1, 0:1],
                                               start=True, stop=True), "mm_t")
            sc.wait("vector", "mm_t")
            sc.do("vector", lambda v: v.tensor_copy(rt128[:, :], pJ0[:, 0:1]), "ev_t2")
            sc.wait("tensor", "ev_t2")
            sc.do("tensor", lambda t: t.matmul(pJ1[:, :], ones1[:], arow[0:1, 8:16], start=True, stop=True),
                  "mm_t2")
            sc.wait("vector", "mm_t2")
            sc.do("vector", lambda v: v.tensor_scalar(rbb[:, :], pJ1[:, :], rt128[:, 0:1], None, ALU.mult),
                  "ev_t3")
            sc.wait("gpsimd", "gp0")
            for c in range(NC):
                sc.do("gpsimd", lambda g, c=c: g.dma_start(out=wpad[c][:], in_=rwd[128 * c:128 * (c + 1), :]),
                      "ldr", 16)
            sc.wait("tensor", "ldr")
            sc.wait("tensor", "ev_t3")
            pq2 = [pA, pB]
            for g_ in range(16):
                p = pq2[g_ % 2]
                for c in range(NC):
                    sc.do("tensor", lambda t, p=p, g_=g_, c=c:
                          t.matmul(p[:, :], xh[c][:, 128 * g_:128 * (g_ + 1)], wpad[c][:],
                                   start=(c == 0), stop=(c == NC - 1)), "mm_r" if c == NC - 1 else None)
                sc.wait("vector", "mm_r")
                sc.do("vector", lambda v, p=p, g_=g_:
                      v.scalar_tensor_tensor(la[:, 8 * g_:8 * (g_ + 1)], p[:, 0:8], rt128[:, 0:1],
                                             rbb[:, :], ALU.mult, ALU.add), "ev_la")
                if g_ >= 1:
                    sc.wait("tensor", "ev_la", sc.val("ev_la") - 1)

        sc.wait("sync", "ev_la")
        sc.do("sync", lambda s: s.dma_start(out=dbgd[:, :], in_=la[:, :]), "stdbg", 16)
        if rcut >= 20:
            for it in range(3):
                for g_ in range(16):
                    slg = slice(8 * g_, 8 * (g_ + 1))
                    sc.do("vector", lambda v, slg=slg: v.reduce_max(m1[:, :], la[:, slg], axis=AX.X, negate=True),
                          "ev_sk")
                    sc.wait("scalar", "ev_sk")
                    sc.do("scalar", lambda s, slg=slg: s.activation(sq_t[:, 0:8], la[:, slg], AF.Exp,
                                                                    bias=m1[:, 0:1]), "s_sk")
                    sc.wait("vector", "s_sk")
                    sc.do("vector", lambda v: v.reduce_sum(s1[:, :], sq_t[:, 0:8].bitcast(BF16)[:, 0:8] if False else sq_t[:, 0:8], axis=AX.X), "ev_sk")
                    sc.wait("scalar", "ev_sk")
                    sc.do("scalar", lambda s: s.activation(l1[:, :], s1[:, :], AF.Ln), "s_sk")
                    sc.wait("vector", "s_sk")
                    sc.do("vector", lambda v, slg=slg: v.tensor_scalar(la[:, slg], la[:, slg], m1[:, 0:1],
                                                                       l1[:, 0:1], ALU.add, ALU.subtract), "ev_sk")
                sc.wait("scalar", "ev_sk")
                sc.do("scalar", lambda s: s.activation(la[:, :], la[:, :], AF.Exp), "s_sk2")
                sc.wait("tensor", "s_sk2")
                sc.do("tensor", lambda t: t.matmul(ps1[:, 0:128], onesKf[:], la[:, :], start=True, stop=True), "mm_sk")
                sc.wait("vector", "mm_sk")
                sc.do("vector", lambda v: v.tensor_reduce(msk[0:1, 0:8],
                                                          ps1[0:1, 0:128].rearrange("p (g e) -> p e g", e=8),
                                                          axis=AX.X, op=ALU.add), "ev_sk2")
                sc.wait("scalar", "ev_sk2")
                sc.do("scalar", lambda s: s.activation(gnorm[0:1, 0:8], msk[0:1, 0:8], AF.Ln), "s_sk3")
                sc.do("scalar", lambda s: s.activation(arow[0:1, 0:8], gnorm[0:1, 0:8], AF.Copy), "s_sk3")
                sc.wait("tensor", "s_sk3")
                sc.do("tensor", lambda t: t.matmul(pR[:, 0:8], ones1[:], arow[0:1, 0:8], start=True, stop=True),
                      "mm_sk2")
                sc.wait("scalar", "mm_sk2")
                sc.do("scalar", lambda s: s.activation(la[:, :], la[:, :], AF.Ln), "s_sk4")
                sc.wait("vector", "s_sk4")
                for g_ in range(16):
                    slg = slice(8 * g_, 8 * (g_ + 1))
                    sc.do("vector", lambda v, slg=slg: v.tensor_sub(la[:, slg], la[:, slg], pR[:, 0:8]), "ev_sk3")
                sc.wait("scalar", "ev_sk3")
                sc.wait("tensor", "ev_sk3")

        if rcut >= 30:
            sc.do("scalar", lambda s: s.activation(la[:, :], la[:, :], AF.Exp), "s_g")
            sc.wait("vector", "s_g")
            _i8 = cidx(1e-8)
            for g_ in range(16):
                slg = slice(8 * g_, 8 * (g_ + 1))
                sc.do("vector", lambda v, slg=slg: v.reduce_sum(s1[:, :], la[:, slg], axis=AX.X), "ev_g")
                sc.wait("vector", "ev_g")
                sc.do("vector", lambda v, _i8=_i8: v.tensor_scalar(s1[:, :], s1[:, :], cb_s[:, _i8:_i8 + 1],
                                                                   None, ALU.add), "ev_g")
                sc.wait("vector", "ev_g")
                sc.do("vector", lambda v: v.reciprocal(s1[:, :], s1[:, :]), "ev_g")
                sc.wait("vector", "ev_g")
                sc.do("vector", lambda v, slg=slg: v.tensor_scalar(gnorm[:, :], la[:, slg], s1[:, 0:1],
                                                                   None, ALU.mult), "ev_g")
                sc.wait("vector", "ev_g")
                sc.do("vector", lambda v: v.reduce_max(m1[:, :], gnorm[:, :], axis=AX.X), "ev_g")
                sc.wait("vector", "ev_g")
                sc.do("vector", lambda v: v.tensor_scalar(msk[:, :], gnorm[:, :], m1[:, 0:1], None,
                                                          ALU.is_ge), "ev_g")
                sc.wait("vector", "ev_g")
                sc.do("vector", lambda v, g_=g_: v.tensor_copy(xs[0][:, 8 * g_:8 * (g_ + 1)], msk[:, :]),
                      "ev_g")
                sc.wait("vector", "ev_g")
                sc.do("vector", lambda v: v.scalar_tensor_tensor(sq_t[:, 0:8].bitcast(F32)[:, 0:8] if False else msk[:, :],
                                                                 msk[:, :], 1.0, msk[:, :], ALU.mult, ALU.bypass), "ev_g") \
                    if False else None
                sc.do("vector", lambda v: v.scalar_tensor_tensor(gnorm[:, :], msk[:, :], -1e30, gnorm[:, :],
                                                                 ALU.mult, ALU.add), "ev_g")
                sc.wait("vector", "ev_g")
                sc.do("vector", lambda v: v.reduce_max(s1[:, :], gnorm[:, :], axis=AX.X), "ev_g")
                sc.wait("vector", "ev_g")
                sc.do("vector", lambda v, slg=slg: v.tensor_scalar(la[:, slg], gnorm[:, :], s1[:, 0:1], None,
                                                                   ALU.is_ge), "ev_g")
                sc.wait("vector", "ev_g")
                sc.do("vector", lambda v, _i8=_i8: v.tensor_scalar(l1[:, :], m1[:, 0:1], s1[:, 0:1],
                                                                   cb_s[:, _i8:_i8 + 1], ALU.add, ALU.add), "ev_g")
                sc.wait("vector", "ev_g")
                sc.do("vector", lambda v: v.reciprocal(l1[:, :], l1[:, :]), "ev_g")
                sc.wait("vector", "ev_g")
                sc.do("vector", lambda v, g_=g_: v.tensor_mul(w1g[:, g_:g_ + 1], m1[:, :], l1[:, :]), "ev_g")
                sc.do("vector", lambda v, g_=g_: v.tensor_mul(w2g[:, g_:g_ + 1], s1[:, :], l1[:, :]), "ev_g")
                sc.do("vector", lambda v, slg=slg: v.tensor_add(msk[:, :], msk[:, :], la[:, slg]), "ev_g")
                sc.wait("vector", "ev_g")
                sc.do("vector", lambda v, g_=g_: v.tensor_copy(sq_t[:, 8 * g_:8 * (g_ + 1)], msk[:, :]), "ev_th")
            sc.wait("tensor", "ev_th")
            sc.do("tensor", lambda t: t.transpose(pTb[:, 0:128], sq_t[:, 0:128], ident[:]), "mm_th")
            sc.wait("vector", "mm_th")
            sc.do("vector", lambda v: v.tensor_copy(thB[:, :], pTb[:, 0:128]), "ev_tht")
            sc.wait("vector", "ev_tht")
            sc.do("vector", lambda v: v.tensor_tensor_scan(cnt2[:, :], thB[:, :], z128[:, :], 0.0,
                                                           ALU.add, ALU.add), "ev_cs")
            sc.wait("vector", "ev_cs")
            sc.do("vector", lambda v: v.tensor_copy(totB[:, :], cnt2[:, 127:128]), "ev_cs")
            sc.wait("vector", "ev_cs")
            sc.do("vector", lambda v: v.tensor_sub(cnt2[:, :], cnt2[:, :], thB[:, :]), "ev_cs")
            sc.wait("tensor", "ev_cs")
            sc.wait("tensor", "ldlm")
            sc.do("tensor", lambda t: t.matmul(pR[:, 0:1], lmat[:, :], totB[:, :], start=True, stop=True),
                  "mm_bp")
            sc.wait("vector", "mm_bp")
            sc.do("vector", lambda v: v.tensor_copy(m1[:, :], pR[:, 0:1]), "ev_bp")
            sc.wait("vector", "ev_bp")
            sc.do("vector", lambda v: v.tensor_scalar(cnt2[:, :], cnt2[:, :], m1[:, 0:1], None, ALU.add),
                  "ev_bp")
            sc.wait("scalar", "ev_bp")
            sc.do("scalar", lambda s: s.activation(thB[:, :], cnt2[:, :], AF.Copy), "s_c2")
            sc.wait("tensor", "s_c2")
            sc.do("tensor", lambda t: t.transpose(pA[:, 0:128], thB[:, :], identf[:, :]), "mm_ct2")
            sc.wait("vector", "mm_ct2")
            sc.do("vector", lambda v: v.tensor_copy(cntT2[:, :], pA[:, 0:128]), "ev_ct2")
            sc.wait("vector", "ev_ct2")
            for g_ in range(16):
                slg = slice(8 * g_, 8 * (g_ + 1))
                sc.do("vector", lambda v, slg=slg: v.scalar_tensor_tensor(rbb[:, :], iotaE[:, :], float(LC),
                                                                          cntT2[:, slg], ALU.mult, ALU.add),
                      "ev_ct")
                sc.wait("vector", "ev_ct")
                for k in range(2):
                    col = 2 * g_ + k
                    oh = (lambda slg=slg: xs[0][:, slg]) if k == 0 else (lambda slg=slg: la[:, slg])
                    sc.do("vector", lambda v, oh=oh, slg=slg: v.tensor_mul(msk[:, :], oh(), cntT2[:, slg]),
                          "ev_ct")
                    sc.do("vector", lambda v, oh=oh: v.tensor_mul(gnorm[:, :], oh(), rbb[:, :]), "ev_ct")
                    sc.wait("vector", "ev_ct")
                    sc.do("vector", lambda v: v.reduce_sum(m1[:, :], msk[:, :], axis=AX.X), "ev_ct")
                    sc.do("vector", lambda v: v.reduce_sum(s1[:, :], gnorm[:, :], axis=AX.X), "ev_ct")
                    sc.wait("vector", "ev_ct")
                    sc.do("vector", lambda v: v.tensor_scalar(l1[:, :], m1[:, :], float(LC), None,
                                                              ALU.is_ge), "ev_ct")
                    sc.wait("vector", "ev_ct")
                    sc.do("vector", lambda v, col=col: v.scalar_tensor_tensor(sl_f[:, col:col + 1], l1[:, :],
                                                                              1e9, s1[:, :], ALU.mult, ALU.add),
                          "ev_ct")
                    sc.wait("vector", "ev_ct")
            sc.wait("vector", "ev_ct")
            sc.do("vector", lambda v: v.tensor_copy(sl_i[:, :], sl_f[:, :]), "ev_sl")

        if rcut >= 40:
            for tc in range(16):
                for c in range(NC):
                    pp = pTb[:, 128 * (c % 4):128 * (c % 4) + 128]
                    sc.do("tensor", lambda t, tc=tc, c=c, pp=pp:
                          t.transpose(pp, xh[c][:, 128 * tc:128 * (tc + 1)], ident[:]), "mm_xt")
                    if c % 4 == 3:
                        sc.wait("vector", "mm_xt")
                        sc.do("vector", lambda v, c=c: v.tensor_copy(xg[0][:, 128 * (c - 3):128 * (c + 1)],
                                                                     pTb[:, 0:512]), "ev_xt")
                        sc.wait("tensor", "ev_xt")
                sc.wait("sync", "ev_xt")
                sc.wait("sync", "gath", max(0, 0))
                sc.do("sync", lambda s, tc=tc: s.dma_start(out=xrows_d[128 * tc:128 * (tc + 1), :], in_=xg[0][:]),
                      "xrw", 16)
                sc.wait("vector", "xrw")
        if rcut >= 50:
            sc.wait("gpsimd", "ev_sl")
            sc.wait("gpsimd", "oob")
            for g_ in range(16):
                for k in range(2):
                    col = 2 * g_ + k
                    sc.do("gpsimd", lambda g, g_=g_, col=col: g.indirect_dma_start(
                        idx_d[:, :], bass.IndirectOffsetOnAxis(sl_i[:, col:col + 1], 0),
                        tokid[:, g_:g_ + 1], None, bounds_check=8 * LC - 1, oob_is_err=False), "sct", 16)
        if rcut == 20:
            sc.wait("sync", "ev_sk3")
            sc.do("sync", lambda s: s.dma_start(out=dbgd[:, :], in_=la[:, :]), "stdbg", 16)
        sc.do("sync", lambda s: s.dma_start(out=sld[:, :], in_=sl_i[:, :]), "stm", 16)
        sc.do("sync", lambda s: s.dma_start(out=w12d[:, 0:16], in_=w1g[:, :]), "stm", 16)
        sc.do("sync", lambda s: s.dma_start(out=w12d[:, 16:32], in_=w2g[:, :]), "stm", 16)
        if rcut >= 40:
            sc.do("sync", lambda s: s.dma_start(out=dbgd[:, :], in_=la[:, :]), "stm", 16)
            sc.do("sync", lambda s: s.dma_start(out=dbg3d[:, :], in_=cntT2[:, :]), "stm", 16)
        sc.wait("gpsimd", "ev_ap")
        for c in range(NC):
            sc.do("gpsimd", lambda g, c=c: g.dma_start(out=xnd[128 * c:128 * (c + 1), :], in_=xh[c][:]),
                  "stm", 16)
        sc.wait("sync", "stm")
        sc.wait("sync", "stdbg")
        sc.wait("sync", "sct")
        sc.wait("sync", "xrw")
        sc.wait("sync", "oob")

    if part in ("exp0", "exp1"):
        ebase = 0 if part == "exp0" else 4
        for e_l in range(4):
            e = ebase + e_l
            sc.do("gpsimd", lambda g, e=e: g.dma_start(out=sl_i[:, 24:32],
                                                       in_=idx_d[LC * e:LC * (e + 1), :].rearrange("(b a) o -> a (b o)", a=128),
                                                       allow_slow_non_contiguous=True),
                  "ldi", 16)
            sc.wait("gpsimd", "ldi")
            for j in range(8):
                sc.wait("gpsimd", "ev_xg", max(0, sc.val("ev_xg") - (8 - 8)))
                sc.do("gpsimd", lambda g, j=j: g.indirect_dma_start(
                    xg[j][:, :], None, xrows_d[:, :], bass.IndirectOffsetOnAxis(sl_i[:, 24 + j:25 + j], 0),
                    bounds_check=T - 1, oob_is_err=False), "gath", 16)
            if e_l == 0:
                sc.wait("sync", "gath")
                sc.do("sync", lambda s: s.dma_start(out=dbgi[:, :], in_=sl_i[:, 24:32]), "dbg", 16)
                sc.do("sync", lambda s: s.dma_start(out=dbgx[:, 0:1024], in_=xg[0][:]), "dbg", 16)
            sc.wait("tensor", "gath")
            for c in range(NC):
                for j in range(8):
                    pp = pTb[:, 128 * (j % 4):128 * (j % 4) + 128]
                    sc.do("tensor", lambda t, c=c, j=j, pp=pp:
                          t.transpose(pp, xg[j][:, 128 * c:128 * (c + 1)], ident[:]), "mm_gt")
                    if j % 4 == 3:
                        sc.wait("vector", "mm_gt")
                        sc.do("vector", lambda v, c=c, j=j: v.tensor_copy(xgT[c][:, 128 * (j - 3):128 * (j + 1)],
                                                                          pTb[:, 0:512]), "ev_gt")
                        sc.wait("tensor", "ev_gt")
            if e_l == 0:
                sc.wait("sync", "ev_gt")
                sc.do("sync", lambda s: s.dma_start(out=dbgx[:, 1024:2048], in_=xgT[0][:, 0:1024]), "dbg", 16)
                sc.wait("gpsimd", "dbg")
            sc.wait("gpsimd", "ev_gt", sc.val("ev_gt"))
            for c in range(NC):
                sc.do("gpsimd", lambda g, e_l=e_l, c=c: g.dma_start(out=wbuf[c][:], in_=wgd[e_l, 128 * c:128 * (c + 1), :]),
                      "ldw", 16)
            sc.wait("tensor", "ldw")
            pgq = [pC, pD]
            npg = [0]

            def ffproj(dst, act, gate_desc=None):
                for fo in range(16):
                    for ct in range(2):
                        p = pgq[npg[0] % 2]
                        for c in range(NC):
                            sc.do("tensor", lambda t, p=p, c=c, fo=fo, ct=ct:
                                  t.matmul(p[:, :], wbuf[c][:, 128 * fo:128 * (fo + 1)],
                                           xgT[c][:, 512 * ct:512 * (ct + 1)], start=(c == 0),
                                           stop=(c == NC - 1)), "mm_f" if c == NC - 1 else None)
                        sc.wait("vector", "mm_f")
                        if act:
                            sc.do("vector", lambda v, p=p, fo=fo, ct=ct:
                                  v.tensor_relu(gT[fo][:, 512 * ct:512 * (ct + 1)], p[:, :]), "ev_f")
                        else:
                            sc.do("vector", lambda v, p=p, fo=fo, ct=ct:
                                  v.tensor_copy(pT[fo][:, 512 * ct:512 * (ct + 1)], p[:, :]), "ev_f")
                        npg[0] += 1
                        sc.wait("tensor", "ev_f", sc.val("ev_f") - 1)
            ffproj(gT, True)
            sc.wait("gpsimd", "mm_f")
            for c in range(NC):
                sc.do("gpsimd", lambda g, e_l=e_l, c=c: g.dma_start(out=wbuf[c][:], in_=wpd[e_l, 128 * c:128 * (c + 1), :]),
                      "ldw", 16)
            sc.wait("tensor", "ldw")
            ffproj(pT, False)
            for fo in range(16):
                sc.do("vector", lambda v, fo=fo: v.tensor_mul(gT[fo][:, :], gT[fo][:, :], pT[fo][:, :]), "ev_h")
            sc.wait("gpsimd", "ev_h", sc.val("ev_h"))
            for fo in range(16):
                sc.do("gpsimd", lambda g, e_l=e_l, fo=fo: g.dma_start(
                    out=wbuf[fo // 2][:, 1024 * (fo % 2):1024 * (fo % 2) + 1024],
                    in_=wod[e_l, 128 * fo:128 * (fo + 1), :]), "ldw2", 16)
            sc.wait("tensor", "ldw2")
            sc.wait("tensor", "ev_h")
            for eo in range(NC):
                for ct in range(2):
                    p = pgq[npg[0] % 2]
                    for fo in range(16):
                        sc.do("tensor", lambda t, p=p, fo=fo, eo=eo, ct=ct:
                              t.matmul(p[:, :], wbuf[fo // 2][:, 1024 * (fo % 2) + 128 * eo:
                                                              1024 * (fo % 2) + 128 * (eo + 1)],
                                       gT[fo][:, 512 * ct:512 * (ct + 1)], start=(fo == 0), stop=(fo == 15)),
                              "mm_f" if fo == 15 else None)
                    sc.wait("vector", "mm_f")
                    sc.do("vector", lambda v, p=p, eo=eo, ct=ct:
                          v.tensor_copy(xgT[eo][:, 512 * ct:512 * (ct + 1)], p[:, :]), "ev_f")
                    npg[0] += 1
                    sc.wait("tensor", "ev_f", sc.val("ev_f") - 1)
            sc.wait("tensor", "ev_f")
            for j in range(8):
                for c in range(NC):
                    pp = pTb[:, 128 * (c % 4):128 * (c % 4) + 128]
                    sc.do("tensor", lambda t, c=c, j=j, pp=pp:
                          t.transpose(pp, xgT[c][:, 128 * j:128 * (j + 1)], ident[:]), "mm_yt")
                    if c % 4 == 3:
                        sc.wait("vector", "mm_yt")
                        sc.do("vector", lambda v, c=c, j=j: v.tensor_copy(xg[j][:, 128 * (c - 3):128 * (c + 1)],
                                                                          pTb[:, 0:512]), "ev_yt")
                        sc.wait("tensor", "ev_yt")
                sc.wait("sync", "ev_yt")
                sc.do("sync", lambda s, e_l=e_l, j=j: s.dma_start(out=yb_d[LC * e_l + 128 * j:LC * e_l + 128 * (j + 1), :],
                                                                  in_=xg[j][:]), "ybw", 16)
            sc.wait("gpsimd", "ybw")
        sc.wait("sync", "ybw")

    if part == "comb":
        sc.do("sync", lambda s: s.dma_start(out=sl_i[:, :], in_=sld[:, :]), "ldm", 16)
        sc.do("sync", lambda s: s.dma_start(out=w1g[:, :], in_=w12d[:, 0:16]), "ldm", 16)
        sc.do("sync", lambda s: s.dma_start(out=w2g[:, :], in_=w12d[:, 16:32]), "ldm", 16)
        for c in range(NC):
            sc.do("sync", lambda s, c=c: s.dma_start(out=xh[c][:], in_=xnd[128 * c:128 * (c + 1), :]),
                  "ldm", 16)
        sc.wait("gpsimd", "ldm")
        sc.wait("vector", "ldm")
        sc.wait("tensor", "ldm")
        for g_ in range(16):
            for k in range(2):
                yy = ytok[k]
                sc.do("vector", lambda v, yy=yy: v.memset(yy[:], 0.0), "ev_z0")
                sc.wait("gpsimd", "ev_z0")
                col = 2 * g_ + k
                sc.do("gpsimd", lambda g, yy=yy, col=col: g.indirect_dma_start(
                    yy[:, :], None, yb_d[:, :], bass.IndirectOffsetOnAxis(sl_i[:, col:col + 1], 0),
                    bounds_check=8 * LC - 1, oob_is_err=False), "gy", 16)
            sc.wait("vector", "gy")
            sc.do("vector", lambda v, g_=g_: v.tensor_scalar(ytok[0][:], ytok[0][:], w1g[:, g_:g_ + 1],
                                                             None, ALU.mult), "ev_c1")
            sc.wait("vector", "ev_c1")
            sc.do("vector", lambda v, g_=g_: v.scalar_tensor_tensor(ytok[0][:], ytok[1][:], w2g[:, g_:g_ + 1],
                                                                    ytok[0][:], ALU.mult, ALU.add), "ev_c2")
            sc.wait("tensor", "ev_c2")
            for c in range(NC):
                pp = pTb[:, 128 * (c % 4):128 * (c % 4) + 128]
                sc.do("tensor", lambda t, c=c, pp=pp:
                      t.transpose(pp, ytok[0][:, 128 * c:128 * (c + 1)], ident[:]), "mm_c")
                if c % 4 == 3:
                    sc.wait("vector", "mm_c")
                    for cc in range(c - 3, c + 1):
                        sc.do("vector", lambda v, cc=cc, g_=g_, c=c:
                              v.tensor_add(xh[cc][:, 128 * g_:128 * (g_ + 1)],
                                           xh[cc][:, 128 * g_:128 * (g_ + 1)],
                                           pTb[:, 128 * (cc % 4):128 * (cc % 4) + 128]), "ev_c3")
                    sc.wait("tensor", "ev_c3")
        nn = [0]
        for eo in range(NC):
            for tt in range(NT4):
                sl = slice(512 * tt, 512 * (tt + 1))
                xst = xs[nn[0] % 2]
                sc.wait("sync", "ev_fy", max(0, sc.val("ev_fy") - 1))
                sc.do("sync", lambda s, xst=xst, eo=eo, sl=sl:
                      s.dma_start(out=xst[:], in_=xTd[128 * eo:128 * (eo + 1), sl]), "ldx4", 16)
                sc.wait("vector", "ldx4")
                sc.wait("vector", "ev_c3", sc.val("ev_c3"))
                sc.wait("vector", "stf")
                sc.do("vector", lambda v, xst=xst, eo=eo, sl=sl:
                      v.tensor_add(yt[:, :], xh[eo][:, sl], xst[:]), "ev_fy")
                sc.wait("sync", "ev_fy")
                sc.do("sync", lambda s, eo=eo, sl=sl: s.dma_start(out=yTd[128 * eo:128 * (eo + 1), sl],
                                                                  in_=yt[:]), "stf", 16)
                nn[0] += 1
        sc.wait("sync", "stf")
    with es:
        nc = sc.build()
    return nc


FFN1_CONSTS = []


def _lmat():
    L = np.zeros((128, 128), np.float32)
    for i in range(128):
        gi, ei = i // 8, i % 8
        for gj in range(gi):
            L[gj * 8 + ei, i] = 1.0
    return L


def run_ffn1(x, inp):
    ncr = _get("ffn1_route", lambda: build_ffn1("route"))
    cv1 = _get("ffn1_consts", lambda: np.asarray(list(FFN1_CONSTS), np.float32))
    nce = _get("ffn1_exp", lambda: build_ffn1("exp0"))
    nce1 = _get("ffn1_exp1", lambda: build_ffn1("exp1"))
    ncc = _get("ffn1_comb", lambda: build_ffn1("comb"))
    rwT = np.asarray(inp["L1_ffn_router_w"], np.float32).T
    rwpad = np.zeros((1024, 512), np.float32)
    rwpad[:, 0:8] = rwT
    rb = np.asarray(inp["L1_ffn_router_b"], np.float32).reshape(1, 8)
    temp = np.asarray(inp["L1_ffn_temperature"], np.float32).reshape(1, 1)
    wg = np.ascontiguousarray(np.asarray(inp["L1_ffn_wg"], np.float32))
    wp = np.ascontiguousarray(np.asarray(inp["L1_ffn_wp"], np.float32))
    wo = np.ascontiguousarray(np.asarray(inp["L1_ffn_wo"], np.float32))
    cv = cv1
    ctab = np.zeros((128, 4), np.float32)
    ctab[:, :len(cv)] = cv[None, :]
    lm = _lmat()
    in_maps = [{"xT": np.ascontiguousarray(x[b].T), "rwpad": rwpad, "rb": rb, "temp": temp,
                "cbt": ctab, "lmat_in": lm} for b in range(8)]
    r1 = run_bass_kernel_spmd(ncr, in_maps, core_ids=list(range(8)))
    outs = []
    ybs = {}
    for half, nch in ((0, nce), (1, nce1)):
        in_maps = []
        for b in range(8):
            in_maps.append({"xrows": r1.results[b]["xrows"], "idxtab": r1.results[b]["idxtab"],
                            "wg": wg[4 * half:4 * half + 4], "wp": wp[4 * half:4 * half + 4],
                            "wo": wo[4 * half:4 * half + 4], "cbt": ctab})
        ybs[half] = run_bass_kernel_spmd(nch, in_maps, core_ids=list(range(8)))
    in_maps = []
    for b in range(8):
        yb = np.concatenate([ybs[0].results[b]["ybufp"], ybs[1].results[b]["ybufp"]], 0)
        in_maps.append({"xT": np.ascontiguousarray(x[b].T), "xnT": r1.results[b]["xnT"],
                        "ybuf": np.ascontiguousarray(yb), "slout": r1.results[b]["slout"],
                        "w12out": r1.results[b]["w12out"], "cbt": ctab})
    r4 = run_bass_kernel_spmd(ncc, in_maps, core_ids=list(range(8)))
    return np.stack([r4.results[b]["yT"].T for b in range(8)], 0).astype(np.float32)


C0 = 5120
F2 = 8192
TT0 = 8 * T
FFN0_CONSTS = []


def build_ffn0(part):
    nc = bass.Bass("TRN2", target_bir_lowering=False)
    if part == "route":
        xTd = nc.dram_tensor("xT", [D, T], F32, kind="ExternalInput").ap()
        rwd = nc.dram_tensor("rwpad", [D, 512], F32, kind="ExternalInput").ap()
        lmd = nc.dram_tensor("lmat_in", [128, 128], F32, kind="ExternalInput").ap()
        gmd = nc.dram_tensor("gam", [128, 8], F32, kind="ExternalInput").ap()
        btd = nc.dram_tensor("bet", [128, 8], F32, kind="ExternalInput").ap()
        emd = nc.dram_tensor("emask", [128, 8], F32, kind="ExternalInput").ap()
        xrows_d = nc.dram_tensor("xrows", [T, D], BF16, kind="ExternalOutput").ap()
        oh1d = nc.dram_tensor("oh1c", [128, 128], F32, kind="ExternalOutput").ap()
        oh2d = nc.dram_tensor("oh2c", [128, 128], F32, kind="ExternalOutput").ap()
        lpd = nc.dram_tensor("lpos", [128, 32], F32, kind="ExternalOutput").ap()
        w12d = nc.dram_tensor("w12out", [128, 32], F32, kind="ExternalOutput").ap()
        totd = nc.dram_tensor("tot", [1, 8], F32, kind="ExternalOutput").ap()
        dbgd = nc.dram_tensor("dbg", [128, 128], F32, kind="ExternalOutput").ap()
    elif part == "slots":
        ttd = nc.dram_tensor("tots", [128, 64], F32, kind="ExternalInput").ap()
        o8d = nc.dram_tensor("cmask", [128, 64], F32, kind="ExternalInput").ap()
        oh1d = nc.dram_tensor("oh1c", [128, 128], F32, kind="ExternalInput").ap()
        oh2d = nc.dram_tensor("oh2c", [128, 128], F32, kind="ExternalInput").ap()
        lpd = nc.dram_tensor("lpos", [128, 32], F32, kind="ExternalInput").ap()
        w12d = nc.dram_tensor("w12out", [128, 32], F32, kind="ExternalInput").ap()
        tkd = nc.dram_tensor("tokid_in", [128, 16], I32, kind="ExternalInput").ap()
        idx_d = nc.dram_tensor("idxtab", [8 * C0, 1], I32, kind="ExternalOutput").ap()
        sld = nc.dram_tensor("slout", [128, 32], I32, kind="ExternalOutput").ap()
        wad = nc.dram_tensor("w12adj", [128, 32], F32, kind="ExternalOutput").ap()
    elif part == "exp":
        idx8d = nc.dram_tensor("idx8", [C0, 8], I32, kind="ExternalInput").ap()
        xrows_d = nc.dram_tensor("xrows_all", [TT0, D], BF16, kind="ExternalInput").ap()
        w12Td = nc.dram_tensor("w12T", [D, F2], F32, kind="ExternalInput").ap()
        w3td = nc.dram_tensor("w3t", [32, 128, D], F32, kind="ExternalInput").ap()
        yb_d = nc.dram_tensor("ybufp", [C0, D], BF16, kind="ExternalOutput").ap()
    else:
        xTd = nc.dram_tensor("xT", [D, T], F32, kind="ExternalInput").ap()
        yb_d = nc.dram_tensor("ybuf", [8 * C0, D], BF16, kind="ExternalInput").ap()
        sld = nc.dram_tensor("slout", [128, 32], I32, kind="ExternalInput").ap()
        wad = nc.dram_tensor("w12adj", [128, 32], F32, kind="ExternalInput").ap()
        yTd = nc.dram_tensor("yT", [D, T], F32, kind="ExternalOutput").ap()
    cbd = nc.dram_tensor("cbt", [128, 4], F32, kind="ExternalInput").ap()

    es = contextlib.ExitStack()
    sb = lambda name, shape, dt=BF16: es.enter_context(nc.sbuf_tensor(name, shape, dt))
    psum = lambda name, shape, dt=F32: es.enter_context(nc.psum_tensor(name, shape, dt))

    cb_s = sb("cbs", [128, 4], F32)
    ones1 = sb("ones1", [1, 128])
    onesK = sb("onesK", [128, 1])
    onesKf = sb("onesKf", [128, 1], F32)
    ident = sb("ident", [128, 128])

    if part == "route":
        xh = [sb(f"xh{c}", [128, T]) for c in range(NC)]
        identf = sb("identf", [128, 128], F32)
        xs = [sb(f"xs{i}", [128, 512], F32) for i in range(2)]
        la = sb("la", [128, 128], F32)
        thB = sb("thB", [128, 128], F32)
        cnt2 = sb("cnt2", [128, 128], F32)
        cntT2 = sb("cntT2", [128, 128], F32)
        z128 = sb("z128", [128, 128], F32)
        totB = sb("totB", [128, 1])
        lmat = sb("lmat", [128, 128])
        m1 = sb("m1", [128, 1], F32)
        s1 = sb("s1", [128, 1], F32)
        l1 = sb("l1", [128, 1], F32)
        ed = sb("ed", [128, 1], F32)
        msk = sb("msk", [128, 8], F32)
        gnorm = sb("gnorm", [128, 8], F32)
        w1g = sb("w1g", [128, 16], F32)
        w2g = sb("w2g", [128, 16], F32)
        sl_f = sb("sl_f", [128, 32], F32)
        gsb = sb("gsb", [128, 8], F32)
        bsb = sb("bsb", [128, 8], F32)
        emk = sb("emk", [128, 8])
        wpad = [sb(f"wpad{c}", [128, 512]) for c in range(NC)]
        xg = [sb("xg0", [128, D])]
        sq_t = sb("sq_t", [128, 512])
        stat2 = sb("stat2", [1, 512], F32)
        arow = sb("arow", [1, 512])
        brow = sb("brow", [1, 512])
        ps1 = psum("ps1", [1, 512])
        ps2 = psum("ps2", [1, 512])
        pA = psum("pA", [128, 512])
        pB = psum("pB", [128, 512])
        pTb = psum("pTb", [128, 512], BF16)
        pR = psum("pR", [128, 8])
    elif part == "slots":
        totsb = sb("totsb", [128, 64], F32)
        o8 = sb("o8", [128, 64], F32)
        t64 = sb("t64", [128, 64], F32)
        oh1s = sb("oh1s", [128, 128], F32)
        oh2s = sb("oh2s", [128, 128], F32)
        lps = sb("lps", [128, 32], F32)
        w12s = sb("w12s", [128, 32], F32)
        tki = sb("tki", [128, 16], I32)
        iotaE = sb("iotaE", [128, 8], F32)
        Pb = sb("Pb", [128, 8], F32)
        rbb = sb("rbb", [128, 8], F32)
        msk = sb("msk", [128, 8], F32)
        gnorm = sb("gnorm", [128, 8], F32)
        m1 = sb("m1", [128, 1], F32)
        m2 = sb("m2", [128, 1], F32)
        s1 = sb("s1", [128, 1], F32)
        l1 = sb("l1", [128, 1], F32)
        sl_f = sb("sl_f", [128, 32], F32)
        sl_i = sb("sl_i", [128, 32], I32)
        wadj = sb("wadj", [128, 32], F32)
        zi = sb("zi", [128, 320], I32)
    elif part == "exp":
        wbuf = [sb(f"w{c}", [128, F2]) for c in range(NC)]
        hT = [sb(f"hT{j}", [128, 512]) for j in range(32)]
        xgT = [sb(f"xgT{c}", [128, 512]) for c in range(NC)]
        xg = [sb(f"xg{q}", [128, D]) for q in range(4)]
        wb3 = [sb(f"wb3{i}", [128, 512]) for i in range(2)]
        yTs = [sb(f"yTs{i}", [128, 512]) for i in range(4)]
        st1 = sb("st1", [128, 512])
        st2 = sb("st2", [128, 512])
        idt = sb("idt", [128, 8], I32)
        idm = sb("idm", [128, 40], I32)
        pC = psum("pC", [128, 512])
        pD = psum("pD", [128, 512])
        pY = [psum(f"pY{i}", [128, 512]) for i in range(4)]
        pTb = psum("pTb", [128, 512], BF16)
    else:
        xh = [sb(f"xh{c}", [128, T]) for c in range(NC)]
        xs = [sb(f"xs{i}", [128, 512], F32) for i in range(2)]
        w1g = sb("w1g", [128, 16], F32)
        w2g = sb("w2g", [128, 16], F32)
        sl_i = sb("sl_i", [128, 32], I32)
        ytok = [sb(f"ytok{i}", [128, D]) for i in range(2)]
        yt = sb("yt", [128, 512], F32)
        sq_t = sb("sq_t", [128, 512])
        pTb = psum("pTb", [128, 512], BF16)

    sc = Sched(nc, es)
    cmap = {}
    FFN0_CONSTS.clear()

    def cidx(v):
        v = float(np.float32(v))
        if v not in cmap:
            cmap[v] = len(FFN0_CONSTS)
            FFN0_CONSTS.append(v)
        return cmap[v]

    sc.do("sync", lambda s: s.dma_start(out=cb_s[:], in_=cbd[:, :]), "ld", 16)
    sc.do("gpsimd", lambda g: g.memset(ones1[:], 1.0))
    sc.do("gpsimd", lambda g: g.memset(onesK[:], 1.0))
    sc.do("gpsimd", lambda g: g.memset(onesKf[:], 1.0), "gpm0")
    if part in ("route", "exp", "comb"):
        src128 = sq_t if part in ("route", "comb") else st1
        sc.do("gpsimd", lambda g: g.memset(src128[:], 1.0), "gpm")
        sc.wait("gpsimd", "gpm")
        sc.do("gpsimd", lambda g: g.affine_select(ident[:], src128[:, 0:128], pattern=[[-1, 128]],
                                                  compare_op=ALU.is_equal, fill=0.0, base=0,
                                                  channel_multiplier=1), "gp")
    if part == "route":
        sc.do("gpsimd", lambda g: g.dma_start(out=lmat[:], in_=lmd[:, :]), "ldlm", 16)
        sc.do("sync", lambda s: s.dma_start(out=gsb[:], in_=gmd[:, :]), "ld", 16)
        sc.do("sync", lambda s: s.dma_start(out=bsb[:], in_=btd[:, :]), "ld", 16)
        sc.do("gpsimd", lambda g: g.dma_start(out=emk[:], in_=emd[:, :]), "ldlm", 16)
        sc.do("gpsimd", lambda g: g.memset(z128[:], 0.0))
        sc.do("gpsimd", lambda g: g.memset(identf[:], 0.0), "gpm")
        sc.do("gpsimd", lambda g: g.memset(la[:], 1.0), "gpm")
        sc.wait("gpsimd", "gpm")
        sc.do("gpsimd", lambda g: g.affine_select(identf[:], la[:, 0:128], pattern=[[-1, 128]],
                                                  compare_op=ALU.is_equal, fill=0.0, base=0,
                                                  channel_multiplier=1), "gp")
    sc.do("gpsimd", lambda g: g.nop(), "gp0")

    if part == "route":
        for tt in range(NT4):
            sl = slice(512 * tt, 512 * (tt + 1))
            if tt > 0:
                sc.wait("sync", "ev_ap")
            for c in range(NC):
                xst = xs[c % 2]
                if tt > 0 or c >= 2:
                    sc.wait("sync", "use_x", sc.val("use_x") - 1)
                sc.do("sync", lambda s, xst=xst, c=c, sl=sl:
                      s.dma_start(out=xst[:], in_=xTd[128 * c:128 * (c + 1), sl]), "ldx", 16)
                sc.wait("scalar", "ldx")
                sc.do("scalar", lambda s, xst=xst: s.activation(sq_t[:], xst[:], AF.Square), "s_sq")
                sc.wait("tensor", "s_sq")
                if tt == 0 and c == 0:
                    sc.wait("tensor", "gp0")
                sc.do("tensor", lambda t, xst=xst, c=c: t.matmul(ps1[:, :], onesKf[:], xst[:],
                                                                 start=(c == 0), stop=(c == NC - 1)))
                sc.do("tensor", lambda t, c=c: t.matmul(ps2[:, :], onesK[:], sq_t[:],
                                                        start=(c == 0), stop=(c == NC - 1)), "mm_st")
                sc.do("tensor", lambda t: t.nop(), "use_x")
                sc.wait("scalar", "mm_st")
            sc.do("scalar", lambda s: s.mul(ps1[0:1, :], ps1[0:1, :], 1.0 / 1024), "s_mu")
            sc.wait("scalar", "s_mu")
            sc.do("scalar", lambda s: s.activation(pA[0:1, :], ps1[0:1, :], AF.Square), "s_m2")
            sc.wait("vector", "mm_st")
            sc.do("vector", lambda v: v.tensor_copy(stat2[:, :], ps2[:, :]), "ev_cp")
            sc.wait("vector", "s_m2")
            sc.do("vector", lambda v: v.scalar_tensor_tensor(stat2[:, :], stat2[:, :], 1.0 / 1024, pA[0:1, :],
                                                             ALU.mult, ALU.subtract), "ev_var")
            sc.wait("scalar", "ev_var")
            _ie = cidx(1e-5)
            sc.do("scalar", lambda s, _ie=_ie: s.activation(stat2[:, :], stat2[:, :], AF.Ln,
                                                            bias=cb_s[0:1, _ie:_ie + 1]), "s_ln")
            sc.wait("scalar", "s_ln")
            sc.do("scalar", lambda s: s.activation(ps2[0:1, :], stat2[:, :], AF.Exp, scale=-0.5), "s_rr")
            sc.wait("scalar", "s_rr")
            sc.do("scalar", lambda s: s.activation(arow[:, :], ps2[0:1, :], AF.Copy), "s_ar")
            sc.wait("vector", "s_ar")
            sc.do("vector", lambda v: v.tensor_scalar(brow[:, :], ps1[0:1, :], -1.0, None, ALU.mult), "ev_ab")
            sc.wait("tensor", "ev_ab")
            sc.do("tensor", lambda t: t.matmul(pA[:, :], ones1[:], brow[:, :], start=True, stop=True))
            sc.do("tensor", lambda t: t.matmul(pB[:, :], ones1[:], arow[:, :], start=True, stop=True), "mm_ab")
            for c in range(NC):
                xst = xs[c % 2]
                sc.wait("sync", "use_x2", sc.val("use_x2") - 1)
                sc.wait("sync", "mm_st")
                sc.do("sync", lambda s, xst=xst, c=c, sl=sl:
                      s.dma_start(out=xst[:], in_=xTd[128 * c:128 * (c + 1), sl]), "ldx2", 16)
                sc.wait("vector", "ldx2")
                if c == 0:
                    sc.wait("vector", "mm_ab")
                sc.do("vector", lambda v, xst=xst: v.tensor_add(sq_t[:, :], xst[:], pA[:, :]))
                sc.do("vector", lambda v, c=c, sl=sl: v.tensor_mul(xh[c][:, sl], sq_t[:, :], pB[:, :]), "ev_ap")
                sc.do("vector", lambda v: v.nop(), "use_x2")
            sc.wait("tensor", "ev_ap")
            sc.wait("scalar", "ev_ap")

        sc.wait("vector", "ld")
        for tt in range(NT4):
            sl = slice(512 * tt, 512 * (tt + 1))
            for c in range(NC):
                sc.wait("scalar", "ev_ap2", sc.val("ev_ap2"))
                sc.do("scalar", lambda s, c=c, sl=sl: s.activation(sq_t[:], xh[c][:, sl], AF.Square), "s_sqb")
                sc.wait("tensor", "s_sqb")
                sc.do("tensor", lambda t, c=c, sl=sl: t.matmul(ps1[:, :], onesK[:], xh[c][:, sl],
                                                               start=(c == 0), stop=(c == NC - 1)))
                sc.do("tensor", lambda t, c=c: t.matmul(ps2[:, :], onesK[:], sq_t[:],
                                                        start=(c == 0), stop=(c == NC - 1)), "mm_stb")
                sc.wait("scalar", "mm_stb")
            sc.do("scalar", lambda s: s.mul(ps1[0:1, :], ps1[0:1, :], 1.0 / 1024), "s_mub")
            sc.wait("scalar", "s_mub")
            sc.do("scalar", lambda s: s.activation(pA[0:1, :], ps1[0:1, :], AF.Square), "s_m2b")
            sc.wait("vector", "mm_stb")
            sc.do("vector", lambda v: v.tensor_copy(stat2[:, :], ps2[:, :]), "ev_cpb")
            sc.wait("vector", "s_m2b")
            sc.do("vector", lambda v: v.scalar_tensor_tensor(stat2[:, :], stat2[:, :], 1.0 / 1024, pA[0:1, :],
                                                             ALU.mult, ALU.subtract), "ev_varb")
            sc.wait("scalar", "ev_varb")
            _ie = cidx(1e-5)
            sc.do("scalar", lambda s, _ie=_ie: s.activation(stat2[:, :], stat2[:, :], AF.Ln,
                                                            bias=cb_s[0:1, _ie:_ie + 1]), "s_lnb")
            sc.wait("scalar", "s_lnb")
            sc.do("scalar", lambda s: s.activation(ps2[0:1, :], stat2[:, :], AF.Exp, scale=-0.5), "s_rrb")
            sc.wait("scalar", "s_rrb")
            sc.do("scalar", lambda s: s.activation(arow[:, :], ps2[0:1, :], AF.Copy), "s_arb")
            sc.wait("vector", "s_arb")
            sc.do("vector", lambda v: v.tensor_scalar(brow[:, :], ps1[0:1, :], -1.0, None, ALU.mult), "ev_abb")
            sc.wait("tensor", "ev_abb")
            sc.do("tensor", lambda t: t.matmul(pA[:, :], ones1[:], brow[:, :], start=True, stop=True))
            sc.do("tensor", lambda t: t.matmul(pB[:, :], ones1[:], arow[:, :], start=True, stop=True), "mm_abb")
            sc.wait("vector", "mm_abb")
            for c in range(NC):
                sc.do("vector", lambda v, c=c, sl=sl: v.tensor_add(sq_t[:, :], xh[c][:, sl], pA[:, :]))
                sc.do("vector", lambda v: v.tensor_mul(sq_t[:, :], sq_t[:, :], pB[:, :]))
                sc.do("vector", lambda v, c=c, sl=sl: v.tensor_scalar(xh[c][:, sl], sq_t[:, :],
                                                                      gsb[:, c:c + 1], bsb[:, c:c + 1],
                                                                      ALU.mult, ALU.add), "ev_ap2")
            sc.wait("tensor", "ev_ap2")
            sc.wait("scalar", "ev_ap2")

        sc.wait("gpsimd", "gp0")
        for c in range(NC):
            sc.do("gpsimd", lambda g, c=c: g.dma_start(out=wpad[c][:], in_=rwd[128 * c:128 * (c + 1), :]),
                  "ldr", 16)
        sc.wait("tensor", "ldr")
        pq2 = [pA, pB]
        for g_ in range(16):
            p = pq2[g_ % 2]
            if g_ >= 2:
                sc.wait("tensor", "ev_la", sc.val("ev_la") - 1)
            for c in range(NC):
                sc.do("tensor", lambda t, p=p, g_=g_, c=c:
                      t.matmul(p[:, :], xh[c][:, 128 * g_:128 * (g_ + 1)], wpad[c][:],
                               start=(c == 0), stop=(c == NC - 1)), "mm_r" if c == NC - 1 else None)
            sc.wait("vector", "mm_r")
            sc.do("vector", lambda v, p=p, g_=g_:
                  v.tensor_scalar(la[:, 8 * g_:8 * (g_ + 1)], p[:, 0:8], 1e4, -1e4,
                                  ALU.min, ALU.max), "ev_la")
        sc.wait("sync", "ev_la")
        sc.do("sync", lambda s: s.dma_start(out=dbgd[:, :], in_=la[:, :]), "stdbg", 16)
        sc.wait("vector", "stdbg")

        for g_ in range(16):
            slg = slice(8 * g_, 8 * (g_ + 1))
            sc.do("vector", lambda v, slg=slg: v.tensor_copy(gnorm[:, :], la[:, slg]), "ev_g")
            sc.wait("vector", "ev_g")
            sc.do("vector", lambda v: v.reduce_max(m1[:, :], gnorm[:, :], axis=AX.X), "ev_g")
            sc.wait("vector", "ev_g")
            sc.do("vector", lambda v: v.tensor_scalar(msk[:, :], gnorm[:, :], m1[:, 0:1], None,
                                                      ALU.is_ge), "ev_g")
            sc.wait("vector", "ev_g")
            sc.do("vector", lambda v, g_=g_: v.tensor_copy(xs[0][:, 8 * g_:8 * (g_ + 1)], msk[:, :]),
                  "ev_g")
            sc.wait("vector", "ev_g")
            sc.do("vector", lambda v: v.scalar_tensor_tensor(gnorm[:, :], msk[:, :], -1e30, gnorm[:, :],
                                                             ALU.mult, ALU.add), "ev_g")
            sc.wait("vector", "ev_g")
            sc.do("vector", lambda v: v.reduce_max(s1[:, :], gnorm[:, :], axis=AX.X), "ev_g")
            sc.wait("vector", "ev_g")
            sc.do("vector", lambda v, slg=slg: v.tensor_scalar(la[:, slg], gnorm[:, :], s1[:, 0:1], None,
                                                               ALU.is_ge), "ev_g")
            sc.do("vector", lambda v: v.tensor_sub(l1[:, :], s1[:, :], m1[:, :]), "ev_g")
            sc.wait("scalar", "ev_g")
            sc.do("scalar", lambda s: s.activation(ed[:, :], l1[:, :], AF.Exp), "s_e")
            sc.wait("vector", "s_e")
            sc.do("vector", lambda v: v.tensor_scalar(l1[:, :], ed[:, :], float(1.0 + 1e-12), None,
                                                      ALU.add), "ev_g")
            sc.wait("vector", "ev_g")
            sc.do("vector", lambda v: v.reciprocal(l1[:, :], l1[:, :]), "ev_g")
            sc.wait("vector", "ev_g")
            sc.do("vector", lambda v, g_=g_: v.tensor_copy(w1g[:, g_:g_ + 1], l1[:, :]), "ev_g")
            sc.do("vector", lambda v, g_=g_: v.tensor_mul(w2g[:, g_:g_ + 1], ed[:, :], l1[:, :]), "ev_g")
            sc.do("vector", lambda v, slg=slg: v.tensor_add(msk[:, :], msk[:, :], la[:, slg]), "ev_g")
            sc.wait("vector", "ev_g")
            sc.do("vector", lambda v, g_=g_: v.tensor_copy(sq_t[:, 8 * g_:8 * (g_ + 1)], msk[:, :]), "ev_th")

        sc.wait("tensor", "ev_th")
        sc.do("tensor", lambda t: t.transpose(pTb[:, 0:128], sq_t[:, 0:128], ident[:]), "mm_th")
        sc.wait("vector", "mm_th")
        sc.do("vector", lambda v: v.tensor_copy(thB[:, :], pTb[:, 0:128]), "ev_tht")
        sc.wait("vector", "ev_tht")
        sc.do("vector", lambda v: v.tensor_tensor_scan(cnt2[:, :], thB[:, :], z128[:, :], 0.0,
                                                       ALU.add, ALU.add), "ev_cs")
        sc.wait("vector", "ev_cs")
        sc.do("vector", lambda v: v.tensor_copy(totB[:, :], cnt2[:, 127:128]), "ev_cs")
        sc.wait("vector", "ev_cs")
        sc.do("vector", lambda v: v.tensor_sub(cnt2[:, :], cnt2[:, :], thB[:, :]), "ev_cs")
        sc.wait("tensor", "ev_cs")
        sc.wait("tensor", "ldlm")
        sc.do("tensor", lambda t: t.matmul(pR[:, 0:1], lmat[:, :], totB[:, :], start=True, stop=True),
              "mm_bp")
        sc.do("tensor", lambda t: t.matmul(ps1[0:1, 0:8], totB[:, :], emk[:, :], start=True, stop=True),
              "mm_tt")
        sc.wait("vector", "mm_tt")
        sc.do("vector", lambda v: v.tensor_copy(stat2[0:1, 0:8], ps1[0:1, 0:8]), "ev_tt")
        sc.wait("sync", "ev_tt")
        sc.do("sync", lambda s: s.dma_start(out=totd[:, :], in_=stat2[0:1, 0:8]), "sttot", 16)
        sc.wait("vector", "mm_bp")
        sc.do("vector", lambda v: v.tensor_copy(m1[:, :], pR[:, 0:1]), "ev_bp")
        sc.wait("vector", "ev_bp")
        sc.do("vector", lambda v: v.tensor_scalar(cnt2[:, :], cnt2[:, :], m1[:, 0:1], None, ALU.add),
              "ev_bp")
        sc.wait("scalar", "ev_bp")
        sc.do("scalar", lambda s: s.activation(thB[:, :], cnt2[:, :], AF.Copy), "s_c2")
        sc.wait("tensor", "s_c2")
        sc.do("tensor", lambda t: t.transpose(pA[:, 0:128], thB[:, :], identf[:, :]), "mm_ct2")
        sc.wait("vector", "mm_ct2")
        sc.do("vector", lambda v: v.tensor_copy(cntT2[:, :], pA[:, 0:128]), "ev_ct2")
        sc.wait("vector", "ev_ct2")
        for g_ in range(16):
            slg = slice(8 * g_, 8 * (g_ + 1))
            for k in range(2):
                col = 2 * g_ + k
                oh = (lambda slg=slg: xs[0][:, slg]) if k == 0 else (lambda slg=slg: la[:, slg])
                sc.do("vector", lambda v, oh=oh, slg=slg: v.tensor_mul(msk[:, :], oh(), cntT2[:, slg]),
                      "ev_ct")
                sc.wait("vector", "ev_ct")
                sc.do("vector", lambda v: v.reduce_sum(m1[:, :], msk[:, :], axis=AX.X), "ev_ct")
                sc.wait("vector", "ev_ct")
                sc.do("vector", lambda v, col=col: v.tensor_copy(sl_f[:, col:col + 1], m1[:, :]), "ev_ct")
                sc.wait("vector", "ev_ct")

        for tc in range(16):
            for c in range(NC):
                pp = pTb[:, 128 * (c % 4):128 * (c % 4) + 128]
                sc.do("tensor", lambda t, tc=tc, c=c, pp=pp:
                      t.transpose(pp, xh[c][:, 128 * tc:128 * (tc + 1)], ident[:]), "mm_xt")
                if c % 4 == 3:
                    sc.wait("vector", "mm_xt")
                    sc.do("vector", lambda v, c=c: v.tensor_copy(xg[0][:, 128 * (c - 3):128 * (c + 1)],
                                                                 pTb[:, 0:512]), "ev_xt")
                    sc.wait("tensor", "ev_xt")
            sc.wait("sync", "ev_xt")
            sc.do("sync", lambda s, tc=tc: s.dma_start(out=xrows_d[128 * tc:128 * (tc + 1), :], in_=xg[0][:]),
                  "xrw", 16)
            sc.wait("vector", "xrw")

        sc.wait("sync", "ev_ct")
        sc.do("sync", lambda s: s.dma_start(out=oh1d[:, :], in_=xs[0][:, 0:128]), "stm", 16)
        sc.do("sync", lambda s: s.dma_start(out=oh2d[:, :], in_=la[:, :]), "stm", 16)
        sc.do("sync", lambda s: s.dma_start(out=lpd[:, :], in_=sl_f[:, :]), "stm", 16)
        sc.do("sync", lambda s: s.dma_start(out=w12d[:, 0:16], in_=w1g[:, :]), "stm", 16)
        sc.do("sync", lambda s: s.dma_start(out=w12d[:, 16:32], in_=w2g[:, :]), "stm", 16)
        sc.wait("sync", "stm")
        sc.wait("sync", "stdbg")
        sc.wait("sync", "sttot")
        sc.wait("sync", "xrw")

    if part == "slots":
        sc.do("sync", lambda s: s.dma_start(out=totsb[:, :], in_=ttd[:, :]), "ld", 16)
        sc.do("sync", lambda s: s.dma_start(out=o8[:, :], in_=o8d[:, :]), "ld", 16)
        sc.do("sync", lambda s: s.dma_start(out=oh1s[:, :], in_=oh1d[:, :]), "ld", 16)
        sc.do("sync", lambda s: s.dma_start(out=oh2s[:, :], in_=oh2d[:, :]), "ld", 16)
        sc.do("sync", lambda s: s.dma_start(out=lps[:, :], in_=lpd[:, :]), "ld", 16)
        sc.do("sync", lambda s: s.dma_start(out=w12s[:, :], in_=w12d[:, :]), "ld", 16)
        sc.do("sync", lambda s: s.dma_start(out=tki[:, :], in_=tkd[:, :]), "ld", 16)
        sc.do("gpsimd", lambda g: g.iota(iotaE[:], pattern=[[1, 8]], base=0, channel_multiplier=0,
                                         allow_small_or_imprecise_dtypes=True), "gp")
        sc.do("gpsimd", lambda g: g.memset(zi[:], 0), "gpz")
        sc.wait("gpsimd", "ld")
        sc.wait("gpsimd", "gpz")
        sc.do("gpsimd", lambda g: g.dma_start(out=idx_d[:, :].rearrange("(a b) o -> a (b o)", a=128),
                                              in_=zi[:, :]), "zidx", 16)
        sc.wait("vector", "ld")
        sc.do("vector", lambda v: v.tensor_mul(t64[:, :], totsb[:, :], o8[:, :]), "ev_pb")
        sc.wait("vector", "ev_pb")
        sc.do("vector", lambda v: v.tensor_reduce(Pb[:, :],
                                                  t64[:, :].rearrange("p (c e) -> p e c", e=8),
                                                  axis=AX.X, op=ALU.add), "ev_pb")
        sc.wait("vector", "ev_pb")
        sc.wait("vector", "gp")
        sc.do("vector", lambda v: v.scalar_tensor_tensor(rbb[:, :], iotaE[:, :], float(C0), Pb[:, :],
                                                         ALU.mult, ALU.add), "ev_pb")
        sc.wait("vector", "ev_pb")
        for g_ in range(16):
            slg = slice(8 * g_, 8 * (g_ + 1))
            for k in range(2):
                col = 2 * g_ + k
                wcol = g_ if k == 0 else 16 + g_
                oh = (lambda slg=slg: oh1s[:, slg]) if k == 0 else (lambda slg=slg: oh2s[:, slg])
                sc.do("vector", lambda v, oh=oh: v.tensor_mul(msk[:, :], oh(), Pb[:, :]), "ev_s")
                sc.do("vector", lambda v, oh=oh: v.tensor_mul(gnorm[:, :], oh(), rbb[:, :]), "ev_s")
                sc.wait("vector", "ev_s")
                sc.do("vector", lambda v: v.reduce_sum(m1[:, :], msk[:, :], axis=AX.X), "ev_s")
                sc.do("vector", lambda v: v.reduce_sum(m2[:, :], gnorm[:, :], axis=AX.X), "ev_s")
                sc.wait("vector", "ev_s")
                sc.do("vector", lambda v, col=col: v.tensor_add(m1[:, :], m1[:, :], lps[:, col:col + 1]),
                      "ev_s")
                sc.do("vector", lambda v, col=col: v.tensor_add(m2[:, :], m2[:, :], lps[:, col:col + 1]),
                      "ev_s")
                sc.wait("vector", "ev_s")
                sc.do("vector", lambda v: v.tensor_scalar(l1[:, :], m1[:, :], float(C0), None,
                                                          ALU.is_ge), "ev_s")
                sc.wait("vector", "ev_s")
                sc.do("vector", lambda v: v.tensor_scalar(s1[:, :], l1[:, :], -1.0, 1.0,
                                                          ALU.mult, ALU.add), "ev_s")
                sc.wait("vector", "ev_s")
                sc.do("vector", lambda v, wcol=wcol: v.tensor_mul(wadj[:, wcol:wcol + 1],
                                                                  w12s[:, wcol:wcol + 1], s1[:, :]), "ev_s")
                sc.do("vector", lambda v, col=col: v.scalar_tensor_tensor(sl_f[:, col:col + 1], l1[:, :],
                                                                          1e9, m2[:, :], ALU.mult, ALU.add),
                      "ev_s")
                sc.wait("vector", "ev_s")
        sc.do("vector", lambda v: v.tensor_copy(sl_i[:, :], sl_f[:, :]), "ev_sl")
        sc.wait("gpsimd", "ev_sl")
        sc.wait("gpsimd", "zidx")
        for g_ in range(16):
            for k in range(2):
                col = 2 * g_ + k
                sc.do("gpsimd", lambda g, g_=g_, col=col: g.indirect_dma_start(
                    idx_d[:, :], bass.IndirectOffsetOnAxis(sl_i[:, col:col + 1], 0),
                    tki[:, g_:g_ + 1], None, bounds_check=8 * C0 - 1, oob_is_err=False), "sct", 16)
        sc.wait("sync", "ev_sl")
        sc.wait("sync", "ev_s")
        sc.do("sync", lambda s: s.dma_start(out=sld[:, :], in_=sl_i[:, :]), "stm", 16)
        sc.do("sync", lambda s: s.dma_start(out=wad[:, :], in_=wadj[:, :]), "stm", 16)
        sc.wait("sync", "stm")
        sc.wait("sync", "sct")

    if part == "exp":
        for r in range(40):
            if r > 0:
                sc.wait("sync", "ev_id", sc.val("ev_id"))
            sc.do("sync", lambda s, r=r: s.dma_start(out=idt[:, :], in_=idx8d[128 * r:128 * (r + 1), :]),
                  "ldi", 16)
            sc.wait("vector", "ldi")

            def red(v, r=r):
                with nc.allow_low_precision(reason="int32 token-id sum is exact"):
                    return v.tensor_reduce(idm[:, r:r + 1], idt[:, :], axis=AX.X, op=ALU.add)
            sc.do("vector", red, "ev_id")
        sc.wait("vector", "ev_id")

        def sub1(v):
            with nc.allow_low_precision(reason="int32 -1 is exact"):
                return v.tensor_scalar(idm[:, :], idm[:, :], -1, None, ALU.add)
        sc.do("vector", sub1, "ev_id")
        sc.wait("gpsimd", "gp0")
        for c in range(NC):
            sc.do("gpsimd", lambda g, c=c: g.dma_start(out=wbuf[c][:], in_=w12Td[128 * c:128 * (c + 1), :]),
                  "ldw", 16)
        sc.wait("gpsimd", "ev_id")
        for t_ in range(10):
            sc.wait("gpsimd", "ybw", sc.val("ybw"))
            for q in range(4):
                sc.do("gpsimd", lambda g, t_=t_, q=q: g.indirect_dma_start(
                    xg[q][:, :], None, xrows_d[:, :],
                    bass.IndirectOffsetOnAxis(idm[:, 4 * t_ + q:4 * t_ + q + 1], 0),
                    bounds_check=TT0 - 1, oob_is_err=False), "gath", 16)
            sc.wait("tensor", "gath")
            sc.wait("tensor", "ev_gu", sc.val("ev_gu"))
            for c in range(NC):
                for q in range(4):
                    pp = pTb[:, 128 * q:128 * q + 128]
                    sc.do("tensor", lambda t, q=q, c=c, pp=pp:
                          t.transpose(pp, xg[q][:, 128 * c:128 * (c + 1)], ident[:]), "mm_gt")
                    if q == 3:
                        sc.wait("vector", "mm_gt")
                        sc.do("vector", lambda v, c=c: v.tensor_copy(xgT[c][:, :], pTb[:, 0:512]), "ev_gt")
                        sc.wait("tensor", "ev_gt")
            sc.wait("tensor", "ev_gt")
            sc.wait("tensor", "ldw")
            for j in range(32):
                sc.wait("tensor", "ev_st2", sc.val("ev_st2"))
                for c in range(NC):
                    sc.do("tensor", lambda t, j=j, c=c:
                          t.matmul(pC[:, :], wbuf[c][:, 128 * j:128 * (j + 1)], xgT[c][:, :],
                                   start=(c == 0), stop=(c == NC - 1)), "mm_g" if c == NC - 1 else None,
                          incn=1)
                sc.wait("tensor", "ev_h", sc.val("ev_h"))
                for c in range(NC):
                    sc.do("tensor", lambda t, j=j, c=c:
                          t.matmul(pD[:, :], wbuf[c][:, 128 * (32 + j):128 * (33 + j)], xgT[c][:, :],
                                   start=(c == 0), stop=(c == NC - 1)), "mm_u" if c == NC - 1 else None,
                          incn=1)
                sc.wait("scalar", "mm_g")
                sc.do("scalar", lambda s: s.activation(st1[:, :], pC[:, :], AF.Sigmoid), "s_sig")
                sc.wait("vector", "s_sig")
                sc.do("vector", lambda v: v.tensor_mul(st2[:, :], st1[:, :], pC[:, :]), "ev_st2")
                sc.wait("vector", "mm_u")
                sc.wait("vector", "ev_st2")
                sc.do("vector", lambda v, j=j: v.tensor_mul(hT[j][:, :], st2[:, :], pD[:, :]), "ev_h")
            sc.do("vector", lambda v: v.nop(), "ev_gu")
            for half in range(2):
                sc.wait("tensor", "ev_y", sc.val("ev_y"))
                for fo in range(32):
                    sc.wait("gpsimd", "mm_y", max(0, sc.val("mm_y") - 1))
                    sc.do("gpsimd", lambda g, t_=t_, fo=fo, half=half: g.dma_start(
                        out=wb3[fo % 2][:, :], in_=w3td[fo, :, 512 * half:512 * (half + 1)]), "ldw3", 16)
                    sc.wait("tensor", "ldw3")
                    if fo == 0:
                        sc.wait("tensor", "ev_h")
                    for eq in range(4):
                        sc.do("tensor", lambda t, fo=fo, eq=eq:
                              t.matmul(pY[eq][:, :], wb3[fo % 2][:, 128 * eq:128 * (eq + 1)], hT[fo][:, :],
                                       start=(fo == 0), stop=(fo == 31)), "mm_y" if eq == 3 else None)
                sc.wait("vector", "mm_y")
                for eq in range(4):
                    sc.do("vector", lambda v, eq=eq: v.tensor_copy(yTs[eq][:, :], pY[eq][:, :]), "ev_y")
                sc.wait("tensor", "ev_y")
                for jq in range(4):
                    for eq in range(4):
                        pp = pTb[:, 128 * eq:128 * eq + 128]
                        sc.do("tensor", lambda t, jq=jq, eq=eq, pp=pp:
                              t.transpose(pp, yTs[eq][:, 128 * jq:128 * (jq + 1)], ident[:]), "mm_yt")
                    sc.wait("vector", "mm_yt")
                    sc.do("vector", lambda v, jq=jq, half=half: v.tensor_copy(
                        xg[jq][:, 512 * half:512 * (half + 1)], pTb[:, 0:512]), "ev_yr")
                    sc.wait("tensor", "ev_yr")
            sc.wait("sync", "ev_yr")
            for jq in range(4):
                sc.do("sync", lambda s, t_=t_, jq=jq: s.dma_start(
                    out=yb_d[512 * t_ + 128 * jq:512 * t_ + 128 * (jq + 1), :], in_=xg[jq][:]), "ybw", 16)
        sc.wait("sync", "ybw")

    if part == "comb":
        sc.do("sync", lambda s: s.dma_start(out=sl_i[:, :], in_=sld[:, :]), "ldm", 16)
        sc.do("sync", lambda s: s.dma_start(out=w1g[:, :], in_=wad[:, 0:16]), "ldm", 16)
        sc.do("sync", lambda s: s.dma_start(out=w2g[:, :], in_=wad[:, 16:32]), "ldm", 16)
        for c in range(NC):
            sc.do("gpsimd", lambda g, c=c: g.memset(xh[c][:], 0.0), "gpz")
        sc.wait("gpsimd", "ldm")
        sc.wait("vector", "ldm")
        sc.wait("vector", "gpz")
        sc.wait("tensor", "gpz")
        for g_ in range(16):
            for k in range(2):
                yy = ytok[k]
                sc.do("vector", lambda v, yy=yy: v.memset(yy[:], 0.0), "ev_z0")
                sc.wait("gpsimd", "ev_z0")
                col = 2 * g_ + k
                sc.do("gpsimd", lambda g, yy=yy, col=col: g.indirect_dma_start(
                    yy[:, :], None, yb_d[:, :], bass.IndirectOffsetOnAxis(sl_i[:, col:col + 1], 0),
                    bounds_check=8 * C0 - 1, oob_is_err=False), "gy", 16)
            sc.wait("vector", "gy")
            sc.do("vector", lambda v, g_=g_: v.tensor_scalar(ytok[0][:], ytok[0][:], w1g[:, g_:g_ + 1],
                                                             None, ALU.mult), "ev_c1")
            sc.wait("vector", "ev_c1")
            sc.do("vector", lambda v, g_=g_: v.scalar_tensor_tensor(ytok[0][:], ytok[1][:], w2g[:, g_:g_ + 1],
                                                                    ytok[0][:], ALU.mult, ALU.add), "ev_c2")
            sc.wait("tensor", "ev_c2")
            for c in range(NC):
                pp = pTb[:, 128 * (c % 4):128 * (c % 4) + 128]
                sc.do("tensor", lambda t, c=c, pp=pp:
                      t.transpose(pp, ytok[0][:, 128 * c:128 * (c + 1)], ident[:]), "mm_c")
                if c % 4 == 3:
                    sc.wait("vector", "mm_c")
                    for cc in range(c - 3, c + 1):
                        sc.do("vector", lambda v, cc=cc, g_=g_:
                              v.tensor_add(xh[cc][:, 128 * g_:128 * (g_ + 1)],
                                           xh[cc][:, 128 * g_:128 * (g_ + 1)],
                                           pTb[:, 128 * (cc % 4):128 * (cc % 4) + 128]), "ev_c3")
                    sc.wait("tensor", "ev_c3")
        nn = [0]
        for eo in range(NC):
            for tt in range(NT4):
                sl = slice(512 * tt, 512 * (tt + 1))
                xst = xs[nn[0] % 2]
                sc.wait("sync", "ev_fy", max(0, sc.val("ev_fy") - 1))
                sc.do("sync", lambda s, xst=xst, eo=eo, sl=sl:
                      s.dma_start(out=xst[:], in_=xTd[128 * eo:128 * (eo + 1), sl]), "ldx4", 16)
                sc.wait("vector", "ldx4")
                sc.wait("vector", "ev_c3", sc.val("ev_c3"))
                sc.wait("vector", "stf")
                sc.do("vector", lambda v, xst=xst, eo=eo, sl=sl:
                      v.tensor_add(yt[:, :], xh[eo][:, sl], xst[:]), "ev_fy")
                sc.wait("sync", "ev_fy")
                sc.do("sync", lambda s, eo=eo, sl=sl: s.dma_start(out=yTd[128 * eo:128 * (eo + 1), sl],
                                                                  in_=yt[:]), "stf", 16)
                nn[0] += 1
        sc.wait("sync", "stf")
    with es:
        nc = sc.build()
    return nc


def run_ffn0(x, inp):
    ncr = _get("ffn0_route", lambda: build_ffn0("route"))
    cv0 = _get("ffn0_consts", lambda: np.asarray(list(FFN0_CONSTS), np.float32))
    ncs = _get("ffn0_slots", lambda: build_ffn0("slots"))
    nce = _get("ffn0_exp", lambda: build_ffn0("exp"))
    ncc = _get("ffn0_comb", lambda: build_ffn0("comb"))
    rw = np.asarray(inp["L0_ffn_router_w"], np.float32)
    rwpad = np.zeros((1024, 512), np.float32)
    rwpad[:, 0:8] = rw.T
    gam = np.ascontiguousarray(np.asarray(inp["L0_ffn_ln_gamma"], np.float32).reshape(8, 128).T)
    bet = np.ascontiguousarray(np.asarray(inp["L0_ffn_ln_beta"], np.float32).reshape(8, 128).T)
    emask = np.ascontiguousarray(np.tile(np.eye(8, dtype=np.float32), (16, 1)))
    w12 = np.asarray(inp["L0_ffn_w12"], np.float32)
    w3 = np.asarray(inp["L0_ffn_w3"], np.float32)
    cv = cv0
    ctab = np.zeros((128, 4), np.float32)
    ctab[:, :len(cv)] = cv[None, :]
    lm = _lmat()
    in_maps = [{"xT": np.ascontiguousarray(x[b].T), "rwpad": rwpad, "lmat_in": lm,
                "gam": gam, "bet": bet, "emask": emask, "cbt": ctab} for b in range(8)]
    r1 = run_bass_kernel_spmd(ncr, in_maps, core_ids=list(range(8)))
    tots = np.concatenate([np.asarray(r1.results[b]["tot"], np.float32) for b in range(8)], 0)
    tots128 = np.ascontiguousarray(np.tile(tots.reshape(1, 64), (128, 1)))
    in_maps = []
    for b in range(8):
        cm = np.zeros((8, 8), np.float32)
        cm[:b, :] = 1.0
        cm = np.ascontiguousarray(np.tile(cm.reshape(1, 64), (128, 1)))
        tk = np.zeros((128, 16), np.int32)
        for g_ in range(16):
            tk[:, g_] = 2048 * b + 128 * g_ + np.arange(128) + 1
        in_maps.append({"tots": tots128, "cmask": cm, "oh1c": r1.results[b]["oh1c"],
                        "oh2c": r1.results[b]["oh2c"], "lpos": r1.results[b]["lpos"],
                        "w12out": r1.results[b]["w12out"], "tokid_in": tk, "cbt": ctab})
    r2 = run_bass_kernel_spmd(ncs, in_maps, core_ids=list(range(8)))
    xrows_all = np.concatenate([np.asarray(r1.results[b]["xrows"]) for b in range(8)], 0)
    in_maps = []
    for e in range(8):
        idx8 = np.ascontiguousarray(np.stack(
            [np.asarray(r2.results[c]["idxtab"][C0 * e:C0 * (e + 1), 0]) for c in range(8)], 1))
        in_maps.append({"idx8": idx8, "xrows_all": xrows_all,
                        "w12T": np.ascontiguousarray(w12[e].T),
                        "w3t": np.ascontiguousarray(w3[e].T).reshape(32, 128, 1024), "cbt": ctab})
    r3 = run_bass_kernel_spmd(nce, in_maps, core_ids=list(range(8)))
    ybuf = np.concatenate([np.asarray(r3.results[e]["ybufp"]) for e in range(8)], 0)
    in_maps = [{"xT": np.ascontiguousarray(x[b].T), "ybuf": ybuf, "slout": r2.results[b]["slout"],
                "w12adj": r2.results[b]["w12adj"], "cbt": ctab} for b in range(8)]
    r4 = run_bass_kernel_spmd(ncc, in_maps, core_ids=list(range(8)))
    return np.stack([r4.results[b]["yT"].T for b in range(8)], 0).astype(np.float32)
